# version 05 (final)
# baseline (speedup 1.0000x reference)
import numpy as np
import concourse.bass as bass
import concourse.mybir as mybir
from concourse.bass_utils import run_bass_kernel_spmd

F32 = mybir.dt.float32
BF16 = mybir.dt.bfloat16
I32 = mybir.dt.int32
AF = mybir.ActivationFunctionType
ALU = mybir.AluOpType
PI = 3.14159265358979323846

NCORES = 8
HALO = 8
NREAL = 2048
WTOK = NREAL + HALO
E = 1024
NCH = E // 128
S = 4096
H = 16
HD = 64
EPS = 1e-5
JW = 8192
JWC = 6144
UBPX = 4672

def mm_ranges(lo, hi, step=512):
    out = []
    a = lo
    while a < hi:
        b = min(a + step, hi)
        out.append((a, b))
        a = b
    return out

MMR = mm_ranges(0, WTOK)
MMR_REAL = mm_ranges(HALO, WTOK)



def make_dft_consts():
    h = np.arange(H)
    d = np.arange(HD)
    hh = np.add.outer(np.multiply.outer(h, h) / H * 0, 0)
    hh2 = np.multiply.outer(h, h) / H
    dd2 = np.multiply.outer(d, d) / HD
    theta = 2 * np.pi * (hh2[:, None, :, None] + dd2[None, :, None, :])
    theta = theta.reshape(E, E)
    C = np.cos(theta)
    Sneg = -np.sin(theta)
    return C.astype(np.float32), Sneg.astype(np.float32)


def make_filter_consts():
    ks = np.fft.fftfreq(S).astype(np.float64)[:, None, None]
    kh = np.fft.fftfreq(H).astype(np.float64)[None, :, None]
    kd = np.fft.fftfreq(HD).astype(np.float64)[None, None, :]
    km = np.sqrt(ks ** 2 + kh ** 2 + kd ** 2 + 1e-10)
    f = np.exp(1j * np.arctan(np.log(km + 1e-10)))
    lam = np.arange(S, dtype=np.float64) / S
    t = np.arange(S, dtype=np.float64) * 0.1
    mod = np.sin(t + lam) * np.cos(t - 2.0 * lam + 0.5 * lam ** 2)
    g = (1.0 + 0.1 * mod) / float(E)
    f = f * g[:, None, None]
    f = f.reshape(S, E)
    return np.ascontiguousarray(f.real.astype(np.float32)), \
        np.ascontiguousarray(f.imag.astype(np.float32))


def make_pext(dist_emb):
    n = dist_emb.shape[0]
    i = np.arange(8191) - 2048
    eff = np.where(i < 0, i + n, i)
    eff = np.clip(eff, 0, n - 1)
    Pext = dist_emb[eff]
    PextT = np.zeros((HD, JW), np.float32)
    PextT[:, :8191] = Pext.T
    PextT_rev = np.zeros((HD, JW), np.float32)
    PextT_rev[:, :8192] = Pext[::-1].T[:, :8192] if False else 0
    rev = np.zeros((8192, HD), np.float32)
    jj = 8191 - np.arange(8192)
    valid = (jj >= 0) & (jj <= 8190)
    rev[valid] = Pext[jj[valid]]
    PextT_rev = np.ascontiguousarray(rev.T.astype(np.float32))
    return np.ascontiguousarray(PextT), PextT_rev


def qkv_col_perm():
    zz, hh, dd = np.meshgrid(np.arange(3), np.arange(H), np.arange(HD),
                             indexing="ij")
    return (hh * 192 + zz * 64 + dd).reshape(-1)


def feat_col(v):
    return np.ascontiguousarray(v.reshape(NCH, 128).T.astype(np.float32))


def feat_col_n(v, n):
    nc = v.shape[0] // 128
    return np.ascontiguousarray(v.reshape(nc, 128).T.astype(np.float32))



ENGS = ("sync", "vector", "scalar", "tensor", "gpsimd")


class KB:

    def __init__(self, nc):
        self.nc = nc
        self.q = {e: [] for e in ENGS}
        self.semdefs = []
        self.cnt = {}
        self.handles = {}
        self.dirty = set()

    def sem(self, name):
        if name not in self.cnt:
            self.semdefs.append(name)
            self.cnt[name] = 0
        return name

    def add(self, eng, fn):
        self.q[eng].append(fn)
        self.dirty.add(eng)

    def bump(self, name, n=1):
        self.cnt[name] += n
        return self.cnt[name]

    def val(self, name):
        return self.cnt[name]

    def wait(self, eng, name, v=None):
        vv = self.cnt[name] if v is None else v
        h = self.handles
        self.q[eng].append(lambda e, _n=name, _v=vv: e.wait_ge(h[_n], _v))

    def op(self, eng, fn, inc=None):
        h = self.handles
        if inc is None:
            self.add(eng, fn)
            return None
        name, k = inc
        v = self.bump(name, k)
        self.add(eng, lambda e, _f=fn, _n=name, _k=k: _f(e).then_inc(h[_n], _k))
        return v

    def mark(self, eng, name, k=1):
        v = self.bump(name, k)
        h = self.handles
        self.add(eng, lambda e, _n=name, _k=k: (e.drain(), e.sem_inc(h[_n], _k)))
        return v

    def dma(self, eng, out_ap, in_ap, sem_name, slow=False):
        v = self.bump(sem_name, 16)
        h = self.handles
        self.add(eng, lambda e, o=out_ap, i=in_ap, _n=sem_name, _s=slow:
                 e.dma_start(out=o, in_=i, allow_slow_non_contiguous=_s)
                 .then_inc(h[_n], 16))
        return v

    def barrier(self, name="PH"):
        self.sem(name)
        h = self.handles
        workers = sorted(self.dirty)
        dma_like = [s for s in self.cnt
                    if (s.startswith("L") or s.startswith("SG") or s == "SC")
                    and self.cnt[s] > 0]
        for e in workers:
            if e == "sync":
                for s in dma_like:
                    self.wait(e, s)
            self.bump(name, 1)
            self.add(e, lambda en, _n=name: (en.drain(), en.sem_inc(h[_n], 1)))
        tgt = self.cnt[name]
        for e in ENGS:
            self.add(e, lambda en, _n=name, _v=tgt: en.wait_ge(h[_n], _v))
        self.dirty = set()

    def finalize(self, stack):
        for name in self.semdefs:
            self.handles[name] = stack.enter_context(self.nc.semaphore(name))
        block = stack.enter_context(self.nc.Block())
        for eng in ENGS:
            closures = self.q[eng]
            if not closures:
                continue
            def runner(e, _c=closures):
                for fn in _c:
                    fn(e)
            getattr(block, eng)(runner)



class Prog:

    def __init__(self, fused=True):
        import contextlib
        self.nc = bass.Bass("TRN2", target_bir_lowering=False)
        self.stack = contextlib.ExitStack()
        self.kb = KB(self.nc)
        self.fused = fused
        self.dram = {}
        self.sb = {}
        self.ps = {}

    def din(self, name, shape, dtype=F32):
        t = self.nc.dram_tensor(name, shape, dtype, kind="ExternalInput")
        self.dram[name] = t.ap()
        return self.dram[name]

    def dout(self, name, shape, dtype=F32):
        t = self.nc.dram_tensor(name, shape, dtype, kind="ExternalOutput")
        self.dram[name] = t.ap()
        return self.dram[name]

    def dscr(self, name, shape, dtype=F32):
        t = self.nc.dram_tensor(name, shape, dtype, kind="Internal")
        self.dram[name] = t.ap()
        return self.dram[name]

    def sbuf(self, name, shape, dtype=F32):
        h = self.stack.enter_context(self.nc.sbuf_tensor(name, shape, dtype))
        self.sb[name] = h
        return h

    def psum(self, name, shape, dtype=F32):
        h = self.stack.enter_context(self.nc.psum_tensor(name, shape, dtype))
        self.ps[name] = h
        return h


def build_pvmap():
    m = {}
    off = 0
    def alloc(name, n):
        nonlocal off
        m[name] = (off, n)
        off += n
    for li in range(4):
        alloc(f"ln1w{li}", 8); alloc(f"ln1b{li}", 8)
        alloc(f"ln2w{li}", 8); alloc(f"ln2b{li}", 8)
        alloc(f"attnb{li}", 24)
        alloc(f"aprojb{li}", 8)
        alloc(f"fcb{li}", 32)
        alloc(f"mprojb{li}", 8)
    alloc("mur", 8); alloc("omur", 8)
    alloc("muk", 8); alloc("omuk", 8)
    alloc("bq", 8); alloc("bk", 8); alloc("bv", 8)
    alloc("flag", 1)
    alloc("eps", 1)
    alloc("one", 1)
    alloc("ebias", 1)
    alloc("zero", 1)
    alloc("halfpi", 1)
    return m, off

PVMAP, PVLEN = build_pvmap()
PV_COLS = 512
assert PVLEN <= PV_COLS


def pv(Psm, name, i=0):
    off, n = PVMAP[name]
    assert i < n
    return Psm[:, off + i:off + i + 1]


def pv_slice(Psm, name, i0, n):
    off, nn = PVMAP[name]
    assert i0 + n <= nn
    return Psm[:, off + i0:off + i0 + n]



class Ctx:
    pass


def setup_common(P):
    c = Ctx()
    c.P = P
    nc, kb = P.nc, P.kb
    c.wa = P.sbuf("wa", [128, 32768], BF16)
    c.ub = P.sbuf("ub", [128, 16448], BF16)
    c.fa = P.sbuf("fa", [128, 10920], F32)
    c.fb = P.sbuf("fb", [128, 5960], F32)
    c.sm = P.sbuf("sm", [128, PV_COLS], F32)
    c.aux = P.sbuf("aux", [128, 1600], F32)
    c.ia = P.sbuf("ia", [128, 1560], I32)
    c.onesc = P.sbuf("onesc", [128, 2], F32)
    c.onesr = P.sbuf("onesr", [1, 128], F32)
    c.wstage = P.sbuf("wstage", [128, 2080], BF16)
    c.rstage = P.sbuf("rstage", [128, 16 * 520], BF16)
    c.astage = P.sbuf("astage", [128, 8 * 520], BF16)
    c.onesrb = P.sbuf("onesrb", [1, 128], BF16)
    c.eyeb = P.sbuf("eyeb", [128, 128], BF16)
    c.eyef_sb = P.sbuf("eyef_sb", [128, 128], F32)
    c.pp = [P.psum(f"pp{i}", [128, 512], F32) for i in range(8)]
    extra = ["LDC", "LDY0", "LDY1", "LDR0", "LDR1", "LDX",
             "SG0", "SG1", "SG2", "SG3", "SG4", "SG5",
             "LW0", "LW1", "LW2", "LW3",
             "LF0", "LF1", "LF2", "LF3",
             "LT0", "LT1", "LR0", "LR1", "LQ0", "LQ1", "LV0", "LV1", "LK0", "LK1"]
    for s in ["LD", "SG", "TT", "VV", "SS", "SS2", "GG", "PH", "SC", "EV", "EV2"] + extra:
        kb.sem(s)
    pvec = P.dram["pvec"]
    eye = P.dram["eyef"]
    kb.dma("sync", c.sm[:, 0:PVLEN], pvec[:, 0:PVLEN], "LD")
    kb.dma("sync", c.fa[:, 0:128], eye[:, :], "LD")
    kb.wait("vector", "LD")
    kb.op("vector", lambda e: e.memset(c.onesc[:, :], 1.0))
    kb.op("vector", lambda e: e.memset(c.onesr[:, :], 1.0))
    kb.op("vector", lambda e: e.memset(c.onesrb[:, :], 1.0))
    kb.op("vector", lambda e: e.tensor_copy(out=c.eyeb[:, :], in_=c.fa[:, 0:128]))
    kb.op("vector", lambda e: e.tensor_copy(out=c.eyef_sb[:, :], in_=c.fa[:, 0:128]),
          inc=("VV", 1))
    kb.barrier()
    return c


def emit_ln(c, src_dram, w=None, b=None, ub_base=0, out_f32_dram=None,
            src_row0=0, bf_scale=None):
    P, kb = c.P, c.P.kb
    CH = 520
    for (a0, a1) in MMR:
        n = a1 - a0
        X = [c.fa[:, ci * CH: ci * CH + n] for ci in range(8)]
        SQ = [c.fb[:, ci * CH: ci * CH + n] for ci in range(8)]
        SUMA = c.fa[:, 4200:4200 + n]
        SUMQ = c.fb[:, 4400:4400 + n]
        mrow = c.aux[0:1, 0:n]
        var = c.aux[0:1, 520:520 + n]
        rstd = c.aux[0:1, 1040:1040 + n]
        kb.wait("sync", "VV")
        kb.wait("sync", "SS")
        kb.wait("scalar", "SG")
        for ci in range(8):
            kb.dma("sync", X[ci],
                   src_dram[src_row0 + ci * 128: src_row0 + (ci + 1) * 128, a0:a1], "LD")
        kb.wait("vector", "LD")
        kb.wait("scalar", "LD")
        kb.wait("scalar", "GG")
        small = n <= 128
        SB_ = c.fa[:, 4720:4720 + n]
        kb.op("vector", lambda e, X=X, o=SUMA: e.tensor_add(out=o, in0=X[0], in1=X[1]))
        kb.op("vector", lambda e, X=X, o=SB_: e.tensor_add(out=o, in0=X[2], in1=X[3]))
        kb.op("vector", lambda e: e.drain())
        kb.op("vector", lambda e, X=X, o=SUMA: e.tensor_add(out=o, in0=o, in1=X[4]))
        kb.op("vector", lambda e, X=X, o=SB_: e.tensor_add(out=o, in0=o, in1=X[5]))
        kb.op("vector", lambda e: e.drain())
        kb.op("vector", lambda e, X=X, o=SUMA: e.tensor_add(out=o, in0=o, in1=X[6]))
        kb.op("vector", lambda e, X=X, o=SB_: e.tensor_add(out=o, in0=o, in1=X[7]))
        kb.op("vector", lambda e: e.drain())
        kb.op("vector", lambda e, o=SUMA, b=SB_: e.tensor_add(out=o, in0=o, in1=b))
        kb.mark("vector", "VV")
        for ci in range(8):
            kb.op("scalar", lambda e, X=X, o=SQ, ci=ci:
                  e.activation(out=o[ci], in_=X[ci], func=AF.Square))
        kb.mark("scalar", "SS")
        kb.wait("gpsimd", "SS")
        QB_ = c.fb[:, 4920:4920 + n]
        kb.op("gpsimd", lambda e, o=SUMQ, a=SQ:
              e.tensor_tensor(out=o, in0=a[0], in1=a[1], op=ALU.add))
        kb.op("gpsimd", lambda e, o=QB_, a=SQ:
              e.tensor_tensor(out=o, in0=a[2], in1=a[3], op=ALU.add))
        kb.op("gpsimd", lambda e: e.drain())
        kb.op("gpsimd", lambda e, o=SUMQ, a=SQ:
              e.tensor_tensor(out=o, in0=o, in1=a[4], op=ALU.add))
        kb.op("gpsimd", lambda e, o=QB_, a=SQ:
              e.tensor_tensor(out=o, in0=o, in1=a[5], op=ALU.add))
        kb.op("gpsimd", lambda e: e.drain())
        kb.op("gpsimd", lambda e, o=SUMQ, a=SQ:
              e.tensor_tensor(out=o, in0=o, in1=a[6], op=ALU.add))
        kb.op("gpsimd", lambda e, o=QB_, a=SQ:
              e.tensor_tensor(out=o, in0=o, in1=a[7], op=ALU.add))
        kb.op("gpsimd", lambda e: e.drain())
        kb.op("gpsimd", lambda e, o=SUMQ, b=QB_:
              e.tensor_tensor(out=o, in0=o, in1=b, op=ALU.add))
        kb.mark("gpsimd", "GG")
        kb.wait("tensor", "VV")
        kb.wait("tensor", "GG")
        kb.wait("tensor", "SS2")
        kb.op("tensor", lambda e, i=SUMA, n=n:
              e.matmul(out=c.pp[6][0:1, :n], lhsT=c.onesc[:, 0:1], rhs=i,
                       start=True, stop=True))
        kb.op("tensor", lambda e, i=SUMQ, n=n:
              e.matmul(out=c.pp[7][0:1, :n], lhsT=c.onesc[:, 0:1], rhs=i,
                       start=True, stop=True), inc=("TT", 1))
        kb.wait("vector", "TT")
        kb.op("vector", lambda e, mrow=mrow, n=n:
              e.tensor_scalar_mul(out=mrow, in0=c.pp[6][0:1, :n], scalar1=1.0 / 1024.0))
        kb.op("vector", lambda e, var=var, n=n:
              e.tensor_scalar_mul(out=var, in0=c.pp[7][0:1, :n], scalar1=1.0 / 1024.0))
        kb.op("vector", lambda e: e.drain())
        kb.op("vector", lambda e, rstd=rstd, mrow=mrow:
              e.tensor_mul(out=rstd, in0=mrow, in1=mrow))
        kb.op("vector", lambda e: e.drain())
        kb.op("vector", lambda e, var=var, rstd=rstd:
              e.tensor_sub(out=var, in0=var, in1=rstd))
        kb.mark("vector", "VV")
        kb.wait("scalar", "VV")
        kb.op("scalar", lambda e, var=var:
              e.activation(out=var, in_=var, func=AF.Sqrt,
                           bias=c.sm[0:1, PVMAP["eps"][0]:PVMAP["eps"][0] + 1]))
        kb.mark("scalar", "SS")
        kb.wait("vector", "SS")
        kb.op("vector", lambda e, rstd=rstd, var=var: e.reciprocal(out=rstd, in_=var))
        kb.op("vector", lambda e: e.drain())
        kb.op("vector", lambda e, mrow=mrow, rstd=rstd:
              e.tensor_mul(out=mrow, in0=mrow, in1=rstd))
        kb.mark("vector", "VV")
        kb.wait("tensor", "VV")
        kb.op("tensor", lambda e, rstd=rstd, n=n:
              e.matmul(out=c.pp[6][:, :n], lhsT=c.onesr[:, :], rhs=rstd,
                       start=True, stop=True))
        kb.op("tensor", lambda e, mrow=mrow, n=n:
              e.matmul(out=c.pp[7][:, :n], lhsT=c.onesr[:, :], rhs=mrow,
                       start=True, stop=True), inc=("TT", 1))
        kb.wait("scalar", "TT")
        kb.wait("scalar", "GG")
        BR = c.fb[:, 4400:4400 + n]
        BMR = c.fa[:, 4200:4200 + n]
        kb.op("scalar", lambda e, BR=BR, n=n:
              e.activation(out=BR, in_=c.pp[6][:, :n], func=AF.Copy))
        kb.op("scalar", lambda e, BMR=BMR, n=n:
              e.activation(out=BMR, in_=c.pp[7][:, :n], func=AF.Copy))
        kb.mark("scalar", "SS2")
        kb.wait("vector", "SS2")
        for ci in range(8):
            kb.op("vector", lambda e, X=X, ci=ci, BR=BR:
                  e.tensor_mul(out=X[ci], in0=X[ci], in1=BR))
        for ci in range(8):
            kb.op("vector", lambda e, X=X, ci=ci, BMR=BMR:
                  e.tensor_sub(out=X[ci], in0=X[ci], in1=BMR))
        if out_f32_dram is not None:
            for ci in range(8):
                kb.op("vector", lambda e, X=X, ci=ci, n=n:
                      e.tensor_copy(out=c.fb[:, ci * CH: ci * CH + n], in_=X[ci]))
        for ci in range(8):
            dstb = c.ub[:, ub_base + ci * WTOK + a0: ub_base + ci * WTOK + a1]
            if w is not None:
                kb.op("vector", lambda e, X=X, ci=ci, d=dstb, w=w, b=b:
                      e.tensor_scalar(out=d, in0=X[ci], scalar1=pv(c.sm, w, ci),
                                      scalar2=pv(c.sm, b, ci), op0=ALU.mult, op1=ALU.add))
            elif bf_scale is not None:
                kb.op("vector", lambda e, X=X, ci=ci, d=dstb, s=bf_scale:
                      e.tensor_scalar_mul(out=d, in0=X[ci], scalar1=s))
            else:
                kb.op("vector", lambda e, X=X, ci=ci, d=dstb: e.tensor_copy(out=d, in_=X[ci]))
        kb.mark("vector", "VV")
        if out_f32_dram is not None:
            kb.wait("sync", "VV")
            for ci in range(8):
                kb.dma("sync", out_f32_dram[ci * 128:(ci + 1) * 128, a0:a1],
                       c.fb[:, ci * CH: ci * CH + n], "SG")
    kb.wait("sync", "VV")


def emit_wload(c, dram_name, row0, nrows, ncols, wa_base, row_stride=None):
    P, kb = c.P, c.P.kb
    src = P.dram[dram_name]
    nch = nrows // 128
    kb.wait("vector", "TT")
    for ci in range(nch):
        stage = c.fa[:, 4800: 4800 + ncols]
        kb.wait("sync", "VV")
        kb.dma("sync", stage, src[row0 + ci * 128: row0 + (ci + 1) * 128, 0:ncols], "LD")
        kb.wait("vector", "LD")
        kb.op("vector", lambda e, s=stage, d=c.wa[:, wa_base + ci * ncols: wa_base + (ci + 1) * ncols]:
              e.tensor_copy(out=d, in_=s), inc=("VV", 1))
    kb.wait("tensor", "VV")


def emit_proj(c, nm, nk, lhsT_ap, rhs_ap, evac, ranges=None, pp_pool=(0, 1, 2, 3, 4, 5)):
    kb = c.P.kb
    ranges = MMR if ranges is None else ranges
    j = 0
    bankfree = {}
    for (a0, a1) in ranges:
        n = a1 - a0
        for m in range(nm):
            pi = pp_pool[j % len(pp_pool)]
            ps = c.pp[pi]
            if pi in bankfree:
                kb.wait("tensor", "EV", bankfree[pi])
            for k in range(nk):
                kb.op("tensor", lambda e, k=k, m=m, a0=a0, a1=a1, ps=ps, n=n, fi=(k == 0), la=(k == nk - 1):
                      e.matmul(out=ps[:, :n], lhsT=lhsT_ap(k, m), rhs=rhs_ap(k, a0, a1),
                               start=fi, stop=la),
                      inc=("TT", 1) if k == nk - 1 else None)
            ttv = kb.val("TT")
            bankfree[pi] = evac(m, a0, a1, ps, ttv)
            j += 1


def evac_to_dram(c, dst_dram, bias_name=None, li=None, func=None, dtype_stage="f32",
                 scale=None):
    P, kb = c.P, c.P.kb
    stages = {}
    state = {"j": 0}

    def evac(m, a0, a1, ps, ttv):
        n = a1 - a0
        j = state["j"]; state["j"] += 1
        si = j % 6
        stage = c.fb[:, si * 520: si * 520 + n]
        kb.wait("scalar", "TT", ttv)
        if si in stages:
            kb.wait("scalar", f"SG{si}", stages[si])
        bias = pv(c.sm, bias_name.format(li=li), m) if bias_name else None
        f = func if func is not None else (AF.Identity if bias is not None else AF.Copy)
        if bias is None:
            kb.op("scalar", lambda e, s=stage, ps=ps, n=n, f=f:
                  e.activation(out=s, in_=ps[:, :n], func=f), inc=("EV", 1))
        else:
            kb.op("scalar", lambda e, s=stage, ps=ps, n=n, bias=bias, f=f:
                  e.activation(out=s, in_=ps[:, :n], func=f, bias=bias),
                  inc=("EV", 1))
        ev = kb.val("EV")
        kb.wait("sync", "EV", ev)
        sg = kb.dma("sync", dst_dram[m * 128:(m + 1) * 128, a0:a1], stage, f"SG{si}")
        stages[si] = sg
        return ev
    return evac


def emit_A1(c, li):
    P, kb = c.P, c.P.kb
    emit_wload(c, "aw", li * 1024, 1024, 3072, 0)
    emit_ln(c, P.dram["y_scr"], w=f"ln1w{li}", b=f"ln1b{li}", ub_base=0)
    kb.barrier()
    evac = evac_to_dram(c, P.dram["qkv_scr"], bias_name="attnb{li}", li=li)
    emit_proj(c, 24, 8,
              lambda k, m: c.wa[:, k * 3072 + m * 128: k * 3072 + (m + 1) * 128],
              lambda k, a0, a1: c.ub[:, k * WTOK + a0: k * WTOK + a1],
              evac)
    kb.barrier()



def stage_x_core(x, core):
    b, h = core // 2, core % 2
    s0 = h * NREAL
    if h == 0:
        seg = np.concatenate([np.zeros((HALO, E), np.float32),
                              x[b, 0:NREAL]], axis=0)
    else:
        seg = x[b, s0 - HALO: s0 + NREAL]
    return np.ascontiguousarray(seg.T.astype(np.float32))


def build_pvec(inp, core):
    pvec = np.zeros((128, PV_COLS), np.float32)
    def put(name, arr2d):
        off, n = PVMAP[name]
        assert arr2d.shape == (128, n), (name, arr2d.shape)
        pvec[:, off:off + n] = arr2d
    perm = qkv_col_perm()
    for li in range(4):
        put(f"ln1w{li}", feat_col(inp["L0_attn_ln1_w"][li]))
        put(f"ln1b{li}", feat_col(inp["L0_attn_ln1_b"][li]))
        put(f"ln2w{li}", feat_col(inp["L0_attn_ln2_w"][li]))
        put(f"ln2b{li}", feat_col(inp["L0_attn_ln2_b"][li]))
        put(f"attnb{li}", feat_col_n(inp["L0_attn_attn_b"][li][perm], 24))
        put(f"aprojb{li}", feat_col(inp["L0_attn_aproj_b"][li]))
        put(f"fcb{li}", feat_col_n(inp["L0_attn_fc_b"][li], 32))
        put(f"mprojb{li}", feat_col(inp["L0_attn_mproj_b"][li]))
    put("mur", feat_col(inp["L0_ffn_mu_r"]))
    put("omur", feat_col(1.0 - inp["L0_ffn_mu_r"]))
    put("muk", feat_col(inp["L0_ffn_mu_k"]))
    put("omuk", feat_col(1.0 - inp["L0_ffn_mu_k"]))
    put("bq", feat_col(inp["L1_attn_bq"]))
    put("bk", feat_col(inp["L1_attn_bk"]))
    put("bv", feat_col(inp["L1_attn_bv"]))
    flag = 0.0 if core % 2 == 0 else 1.0
    pvec[:, PVMAP["flag"][0]] = flag
    pvec[:, PVMAP["eps"][0]] = EPS
    pvec[:, PVMAP["halfpi"][0]] = PI / 2.0
    pvec[:, PVMAP["one"][0]] = 1.0
    eb = np.zeros(128, np.float32)
    eb[:16] = inp["L1_ffn_e_bias"]
    pvec[:, PVMAP["ebias"][0]] = eb
    return pvec


TQ = 514


def emit_cos3(c, pairs):
    kb = c.P.kb
    t1s = [c.fb[:, 4400 + i * 520: 4400 + i * 520 + TQ] for i in range(3)]
    ias = [c.ia[:, i * 520: i * 520 + TQ] for i in range(3)]
    srcs = [se[:, 1:TQ + 1] for (_, se) in pairs]
    dsts = [d for (d, _) in pairs]
    for i in range(3):
        kb.op("vector", lambda e, t=t1s[i], s=srcs[i]:
              e.tensor_scalar_mul(out=t, in0=s, scalar1=1.0 / (2 * PI)))
    for i in range(3):
        kb.op("vector", lambda e, ia=ias[i], t=t1s[i]: e.tensor_copy(out=ia, in_=t))
    for i in range(3):
        kb.op("vector", lambda e, ia=ias[i], t=t1s[i]: e.tensor_copy(out=t, in_=ia))
    for i in range(3):
        kb.op("vector", lambda e, d=dsts[i], t=t1s[i], s=srcs[i]:
              e.scalar_tensor_tensor(out=d, in0=t, scalar=-2 * PI, in1=s,
                                     op0=ALU.mult, op1=ALU.add))
    kb.mark("vector", "VV")
    kb.wait("scalar", "VV")
    for i in range(3):
        kb.op("scalar", lambda e, d=dsts[i]: e.activation(out=d, in_=d, func=AF.Abs))
    for i in range(3):
        kb.op("scalar", lambda e, d=dsts[i]:
              e.activation(out=d, in_=d, func=AF.Sin, bias=pv(c.sm, "halfpi"), scale=-1.0))
    kb.mark("scalar", "SS")
    kb.wait("vector", "SS")
    kb.wait("gpsimd", "SS")
    kb.wait("gpsimd", "VV")


def emit_A2(c, li):
    P, kb = c.P, c.P.kb
    qkv = P.dram["qkv_scr"]
    FA = 520
    def T(i, w=TQ):
        return c.fa[:, i * FA: i * FA + w]
    Q0e, K0e, V0e = T(0, TQ + 1), T(1, TQ + 1), T(2, TQ + 1)
    Q2, K2, V2 = T(3), T(4), T(5)
    Q1, K1, V1 = T(6), T(7), T(8)
    Q3, K3, V3 = T(9), T(10), T(11)
    QK = [T(12), T(13), T(14), T(15)]
    TA, TB, TC, TD = T(16), T(17), T(18), T(19)
    M, A_, S_ = ALU.mult, ALU.add, ALU.subtract
    for ci in range(8):
        for ti in range(4):
            t0 = ti * TQ
            kb.barrier()
            for (z, Ze) in ((0, Q0e), (1, K0e), (2, V0e)):
                r0 = z * 1024 + ci * 128
                if t0 == 0:
                    kb.dma("sync", Ze[:, 1:TQ + 1], qkv[r0:r0 + 128, 0:TQ], "LD")
                    kb.dma("sync", Ze[:, 0:1], qkv[r0:r0 + 128, 0:1], "LD", slow=True)
                else:
                    kb.dma("sync", Ze[:, 0:TQ + 1], qkv[r0:r0 + 128, t0 - 1:t0 + TQ], "LD")
            for (z, Z2) in ((0, Q2), (1, K2), (2, V2)):
                base = z * 1024
                lo = (ci * 128 - 64) % 1024
                if ci == 0:
                    kb.dma("sync", Z2[0:64, :], qkv[base + 960:base + 1024, t0:t0 + TQ], "LD")
                    kb.dma("sync", Z2[64:128, :], qkv[base + 0:base + 64, t0:t0 + TQ], "LD")
                else:
                    kb.dma("sync", Z2[:, :], qkv[base + lo:base + lo + 128, t0:t0 + TQ], "LD")
            kb.wait("vector", "LD")
            kb.wait("scalar", "LD")
            kb.wait("gpsimd", "LD")
            for (Ze, Z1) in ((Q0e, Q1), (K0e, K1), (V0e, V1)):
                kb.op("vector", lambda e, Ze=Ze, Z1=Z1:
                      e.tensor_sub(out=Z1, in0=Ze[:, 1:TQ + 1], in1=Ze[:, 0:TQ]))
            if ti == 0:
                for Z1 in (Q1, K1, V1):
                    kb.op("vector", lambda e: e.drain())
                    kb.op("vector", lambda e, Z1=Z1:
                          e.tensor_scalar_mul(out=Z1[:, 8:9], in0=Z1[:, 8:9],
                                              scalar1=pv(c.sm, "flag")))
                kb.op("vector", lambda e: e.drain())
            emit_cos3(c, [(Q3, Q0e), (K3, K0e), (V3, V0e)])
            q0 = Q0e[:, 1:TQ + 1]; k0 = K0e[:, 1:TQ + 1]; v0 = V0e[:, 1:TQ + 1]
            def dve(out, a, b, op):
                kb.op("vector", lambda e, out=out, a=a, b=b, op=op:
                      e.tensor_tensor(out=out, in0=a, in1=b, op=op))
            def pool(out, a, b, op):
                kb.op("gpsimd", lambda e, out=out, a=a, b=b, op=op:
                      e.tensor_tensor(out=out, in0=a, in1=b, op=op))
            dve(QK[0], q0, k0, M); dve(QK[1], q0, K1, M)
            dve(TA, Q1, K1, M);    dve(TB, Q1, k0, M)
            dve(QK[0], QK[0], TA, S_); dve(QK[1], QK[1], TB, A_)
            dve(TA, Q2, K2, M);    dve(TB, Q2, K3, M)
            dve(QK[0], QK[0], TA, S_); dve(QK[1], QK[1], TB, A_)
            dve(TA, Q3, K3, M);    dve(TB, Q3, K2, M)
            dve(QK[0], QK[0], TA, S_); dve(QK[1], QK[1], TB, S_)
            pool(QK[2], q0, K2, M); pool(QK[3], q0, K3, M)
            pool(TC, Q1, K3, M);    pool(TD, Q1, K2, M)
            pool(QK[2], QK[2], TC, S_); pool(QK[3], QK[3], TD, A_)
            pool(TC, Q2, k0, M);    pool(TD, Q2, K1, M)
            pool(QK[2], QK[2], TC, A_); pool(QK[3], QK[3], TD, S_)
            pool(TC, Q3, K1, M);    pool(TD, Q3, k0, M)
            pool(QK[2], QK[2], TC, A_); pool(QK[3], QK[3], TD, A_)
            kb.mark("gpsimd", "GG")
            dve(TA, QK[0], v0, M); dve(TB, QK[1], V1, M)
            kb.op("vector", lambda e: e.drain())
            dve(TA, TA, TB, S_)
            pool(TC, QK[2], V2, M); pool(TD, QK[3], V3, M)
            pool(TC, TC, TD, A_)
            kb.mark("gpsimd", "GG")
            kb.wait("vector", "GG")
            dstb = c.ub[:, ci * WTOK + t0: ci * WTOK + t0 + TQ]
            kb.op("vector", lambda e: e.drain())
            kb.op("vector", lambda e, d=dstb, TA=TA, TC=TC: e.tensor_sub(out=d, in0=TA, in1=TC))
            kb.mark("vector", "VV")
    kb.barrier()


def emit_A3(c, li, upto=99):
    P, kb = c.P, c.P.kb
    emit_wload(c, "Cm", 0, 1024, 1024, 0)
    emit_wload(c, "Sm", 0, 1024, 1024, 8192)
    emit_wload(c, "apw", li * 1024, 1024, 1024, 16384)
    kb.barrier()
    if upto < 1:
        return
    wri = P.dram["wri_scr"]
    frt, fit = P.dram["frt"], P.dram["fit"]
    y = P.dram["y_scr"]

    FR = [c.fa[:, i * 520: i * 520 + 512] for i in range(4)]
    T1 = [c.fa[:, (4 + i) * 520: (4 + i) * 520 + 512] for i in range(8)]
    WS = [c.wstage[:, i * 520: i * 520 + 512] for i in range(4)]
    j = 0
    ev_hist = {}
    sg_hist = {}
    ld_hist = {}
    for m in range(8):
        for (a0, a1) in MMR:
            n = a1 - a0
            rot = j % 2
            psA, psB = c.pp[2 * rot], c.pp[2 * rot + 1]
            frS, fiS = FR[2 * rot], FR[2 * rot + 1]
            t1S, t2S, t3S, t4S = T1[4 * rot], T1[4 * rot + 1], T1[4 * rot + 2], T1[4 * rot + 3]
            wreS, wimS = WS[2 * rot], WS[2 * rot + 1]
            if rot in ld_hist:
                kb.wait("sync", "EV", ld_hist[rot])
            kb.dma("sync", frS[:, :n], frt[m * 128:(m + 1) * 128, a0:a1], f"LF{rot}")
            ldv = kb.dma("sync", fiS[:, :n], fit[m * 128:(m + 1) * 128, a0:a1], f"LF{rot}")
            if rot in ev_hist:
                kb.wait("tensor", "EV", ev_hist[rot])
            for k in range(8):
                kb.op("tensor", lambda e, k=k, m=m, a0=a0, a1=a1, ps=psA, n=n:
                      e.matmul(out=ps[:, :n], lhsT=c.wa[:, k * 1024 + m * 128: k * 1024 + m * 128 + 128],
                               rhs=c.ub[:, k * WTOK + a0: k * WTOK + a1],
                               start=(k == 0), stop=(k == 7)))
            for k in range(8):
                kb.op("tensor", lambda e, k=k, m=m, a0=a0, a1=a1, ps=psB, n=n:
                      e.matmul(out=ps[:, :n], lhsT=c.wa[:, 8192 + k * 1024 + m * 128: 8192 + k * 1024 + m * 128 + 128],
                               rhs=c.ub[:, k * WTOK + a0: k * WTOK + a1],
                               start=(k == 0), stop=(k == 7)),
                      inc=("TT", 1) if k == 7 else None)
            ttv = kb.val("TT")
            kb.wait("vector", "TT", ttv)
            kb.wait("vector", f"LF{rot}", ldv)
            if rot in sg_hist:
                kb.wait("vector", f"SG{4 + rot}", sg_hist[rot])
            kb.op("vector", lambda e, o=t1S, a=frS, ps=psA, n=n:
                  e.tensor_tensor(out=o[:, :n], in0=a[:, :n], in1=ps[:, :n], op=ALU.mult))
            kb.op("vector", lambda e, o=t2S, a=fiS, ps=psB, n=n:
                  e.tensor_tensor(out=o[:, :n], in0=a[:, :n], in1=ps[:, :n], op=ALU.mult))
            kb.op("vector", lambda e, o=t3S, a=frS, ps=psB, n=n:
                  e.tensor_tensor(out=o[:, :n], in0=a[:, :n], in1=ps[:, :n], op=ALU.mult))
            kb.op("vector", lambda e, o=t4S, a=fiS, ps=psA, n=n:
                  e.tensor_tensor(out=o[:, :n], in0=a[:, :n], in1=ps[:, :n], op=ALU.mult))
            kb.op("vector", lambda e, o=wreS, a=t1S, b=t2S, n=n:
                  e.tensor_sub(out=o[:, :n], in0=a[:, :n], in1=b[:, :n]))
            kb.op("vector", lambda e, o=wimS, a=t3S, b=t4S, n=n:
                  e.tensor_add(out=o[:, :n], in0=a[:, :n], in1=b[:, :n]))
            kb.mark("vector", "EV")
            evv = kb.val("EV")
            ev_hist[rot] = evv
            ld_hist[rot] = evv
            kb.wait("sync", "EV", evv)
            kb.dma("sync", wri[m * 128:(m + 1) * 128, a0:a1], wreS[:, :n], f"SG{4 + rot}")
            sgv = kb.dma("sync", wri[1024 + m * 128: 1024 + (m + 1) * 128, a0:a1], wimS[:, :n], f"SG{4 + rot}")
            sg_hist[rot] = sgv
            j += 1
    kb.barrier()
    if upto < 2:
        return

    RS = [c.rstage[:, i * 520: i * 520 + 512] for i in range(16)]
    AS = [c.astage[:, i * 520: i * 520 + 512] for i in range(8)]
    FS = [c.fa[:, i * 520: i * 520 + 512] for i in range(2)]
    last_inv_tt = 0
    ev2_hist = []
    fs_sg = {}
    for ri, (a0, a1) in enumerate(MMR):
        n = a1 - a0
        kb.wait("sync", "TT", last_inv_tt)
        for k in range(8):
            kb.dma("sync", RS[k][:, :n], wri[k * 128:(k + 1) * 128, a0:a1], "LD")
        for k in range(8):
            kb.dma("sync", RS[8 + k][:, :n], wri[1024 + k * 128:1024 + (k + 1) * 128, a0:a1], "LD")
        kb.wait("sync", "EV")
        for mo in range(8):
            kb.dma("sync", c.fb[:, mo * 520: mo * 520 + n],
                   y[mo * 128:(mo + 1) * 128, a0:a1], "LD")
        ldv = kb.val("LD")
        kb.wait("tensor", "LD", ldv)
        for m in range(8):
            ps = c.pp[m % 4]
            if len(ev2_hist) >= 4:
                kb.wait("tensor", "EV2", ev2_hist[-4])
            for k in range(8):
                kb.op("tensor", lambda e, k=k, m=m, ps=ps, n=n:
                      e.matmul(out=ps[:, :n], lhsT=c.wa[:, k * 1024 + m * 128: k * 1024 + m * 128 + 128],
                               rhs=RS[k][:, :n], start=(k == 0), stop=False))
            for k in range(8):
                kb.op("tensor", lambda e, k=k, m=m, ps=ps, n=n:
                      e.matmul(out=ps[:, :n], lhsT=c.wa[:, 8192 + k * 1024 + m * 128: 8192 + k * 1024 + m * 128 + 128],
                               rhs=RS[8 + k][:, :n], start=False, stop=(k == 7)),
                      inc=("TT", 1) if k == 7 else None)
            ttv = kb.val("TT")
            kb.wait("scalar", "TT", ttv)
            kb.op("scalar", lambda e, m=m, ps=ps, n=n:
                  e.activation(out=AS[m][:, :n], in_=ps[:, :n], func=AF.Copy))
            ev2_hist.append(kb.mark("scalar", "EV2"))
        last_inv_tt = kb.val("TT")
        kb.wait("tensor", "EV2")
        for mo in range(8):
            ps2 = c.pp[4 + mo % 2]
            if mo >= 2:
                kb.wait("tensor", "EV", kb.val("EV") - 1)
            for k in range(8):
                kb.op("tensor", lambda e, k=k, mo=mo, ps=ps2, n=n:
                      e.matmul(out=ps[:, :n], lhsT=c.wa[:, 16384 + k * 1024 + mo * 128: 16384 + k * 1024 + mo * 128 + 128],
                               rhs=AS[k][:, :n], start=(k == 0), stop=(k == 7)),
                      inc=("TT", 1) if k == 7 else None)
            ttv = kb.val("TT")
            kb.wait("scalar", "TT", ttv)
            fS = FS[mo % 2]
            if fs_sg.get(mo % 2):
                kb.wait("scalar", "SG", fs_sg[mo % 2])
            kb.op("scalar", lambda e, ps=ps2, n=n, fS=fS, mo=mo:
                  e.activation(out=fS[:, :n], in_=ps[:, :n], func=AF.Identity,
                               bias=pv(c.sm, f"aprojb{li}", mo)))
            kb.mark("scalar", "SS2")
            ssv = kb.val("SS2")
            kb.wait("vector", "SS2", ssv)
            kb.wait("vector", "LD", ldv)
            kb.op("vector", lambda e, fS=fS, mo=mo, n=n:
                  e.tensor_add(out=fS[:, :n], in0=fS[:, :n],
                               in1=c.fb[:, mo * 520: mo * 520 + n]))
            kb.mark("vector", "EV")
            evv = kb.val("EV")
            kb.wait("sync", "EV", evv)
            fs_sg[mo % 2] = kb.dma("sync", y[mo * 128:(mo + 1) * 128, a0:a1], fS[:, :n], "SG")
    kb.barrier()


def emit_A4(c, li):
    P, kb = c.P, c.P.kb
    emit_wload(c, "fcw", li * 1024, 1024, 4096, 0)
    emit_ln(c, P.dram["y_scr"], w=f"ln2w{li}", b=f"ln2b{li}", ub_base=0)
    kb.barrier()
    h = P.dram["h_scr"]
    WS = [c.wstage[:, i * 520: i * 520 + 512] for i in range(4)]
    state = {"j": 0}
    sg_hist = {}

    def evac(m, a0, a1, ps, ttv):
        n = a1 - a0
        j = state["j"]; state["j"] += 1
        si = j % 4
        stage = WS[si]
        kb.wait("scalar", "TT", ttv)
        if si in sg_hist:
            kb.wait("scalar", f"SG{si}", sg_hist[si])
        kb.op("scalar", lambda e, s=stage, ps=ps, n=n, m=m:
              e.activation(out=s[:, :n], in_=ps[:, :n], func=AF.Gelu,
                           bias=pv(c.sm, f"fcb{li}", m)))
        kb.mark("scalar", "EV")
        ev = kb.val("EV")
        kb.wait("sync", "EV", ev)
        sg_hist[si] = kb.dma("sync", h[m * 128:(m + 1) * 128, a0:a1], stage[:, :n], f"SG{si}")
        return ev

    emit_proj(c, 32, 8,
              lambda k, m: c.wa[:, k * 4096 + m * 128: k * 4096 + m * 128 + 128],
              lambda k, a0, a1: c.ub[:, k * WTOK + a0: k * WTOK + a1],
              evac)
    kb.barrier()


def emit_A5(c, li):
    P, kb = c.P, c.P.kb
    emit_wload(c, "mpw", li * 4096, 4096, 1024, 0)
    kb.barrier()
    h = P.dram["h_scr"]
    y = P.dram["y_scr"]
    RS = [c.rstage[:, i * 520: i * 520 + 512] for i in range(16)]
    FS = [c.fa[:, i * 520: i * 520 + 512] for i in range(2)]
    fs_sg = {}
    last_tt = 0
    for (a0, a1) in MMR:
        n = a1 - a0
        kb.wait("sync", "EV")
        for mo in range(8):
            kb.dma("sync", c.fb[:, mo * 520: mo * 520 + n],
                   y[mo * 128:(mo + 1) * 128, a0:a1], "LDX")
        ldy = kb.val("LDX")
        for half in range(2):
            kb.wait("sync", "TT", last_tt)
            for k in range(16):
                kb.dma("sync", RS[k][:, :n],
                       h[(half * 16 + k) * 128:(half * 16 + k + 1) * 128, a0:a1], "LD")
            ldv = kb.val("LD")
            kb.wait("tensor", "LD", ldv)
            for m in range(8):
                for k in range(16):
                    kk = half * 16 + k
                    kb.op("tensor", lambda e, k=k, kk=kk, m=m, n=n, half=half:
                          e.matmul(out=c.pp[m][:, :n],
                                   lhsT=c.wa[:, kk * 1024 + m * 128: kk * 1024 + m * 128 + 128],
                                   rhs=RS[k][:, :n],
                                   start=(kk == 0), stop=(kk == 31)),
                          inc=("TT", 1) if (k == 15) else None)
            last_tt = kb.val("TT")
        for mo in range(8):
            fS = FS[mo % 2]
            kb.wait("scalar", "TT", last_tt)
            if fs_sg.get(mo % 2):
                kb.wait("scalar", "SG", fs_sg[mo % 2])
            kb.op("scalar", lambda e, ps=c.pp[mo], n=n, fS=fS, mo=mo:
                  e.activation(out=fS[:, :n], in_=ps[:, :n], func=AF.Identity,
                               bias=pv(c.sm, f"mprojb{li}", mo)))
            kb.mark("scalar", "SS2")
            kb.wait("vector", "SS2")
            kb.wait("vector", "LDX", ldy)
            kb.op("vector", lambda e, fS=fS, mo=mo, n=n:
                  e.tensor_add(out=fS[:, :n], in0=fS[:, :n],
                               in1=c.fb[:, mo * 520: mo * 520 + n]))
            kb.mark("vector", "EV")
            kb.wait("gpsimd", "EV")
            fs_sg[mo % 2] = kb.dma("gpsimd", y[mo * 128:(mo + 1) * 128, a0:a1],
                                   fS[:, :n], "SG")
    kb.barrier()


def emit_residual_xy(c, dst_name, a_name, b_name):
    P, kb = c.P, c.P.kb
    A, B, D = P.dram[a_name], P.dram[b_name], P.dram[dst_name]
    for (a0, a1) in MMR:
        n = a1 - a0
        kb.wait("sync", "VV")
        kb.wait("sync", "SG")
        for ci in range(8):
            kb.dma("sync", c.fa[:, ci * 520: ci * 520 + n], A[ci * 128:(ci + 1) * 128, a0:a1], "LD")
            kb.dma("sync", c.fb[:, ci * 520: ci * 520 + n], B[ci * 128:(ci + 1) * 128, a0:a1], "LD")
        kb.wait("vector", "LD")
        for ci in range(8):
            kb.op("vector", lambda e, ci=ci, n=n:
                  e.tensor_add(out=c.fa[:, ci * 520: ci * 520 + n],
                               in0=c.fa[:, ci * 520: ci * 520 + n],
                               in1=c.fb[:, ci * 520: ci * 520 + n]))
        kb.mark("vector", "VV")
        kb.wait("sync", "VV")
        for ci in range(8):
            kb.dma("sync", D[ci * 128:(ci + 1) * 128, a0:a1],
                   c.fa[:, ci * 520: ci * 520 + n], "SG")
    kb.wait("sync", "SG")
    kb.barrier()


def emit_PB(c, upto=99):
    P, kb = c.P, c.P.kb
    emit_wload(c, "wrT", 0, 1024, 1024, 0)
    emit_wload(c, "wkT", 0, 1024, 1024, 8192)
    emit_wload(c, "wvT", 0, 1024, 1024, 16384)
    emit_ln(c, P.dram["x_scr"], out_f32_dram=P.dram["u_scr"])
    kb.barrier()
    u = P.dram["u_scr"]
    x = P.dram["x_scr"]
    rr = P.dram["r_scr"]

    def build_mix(mu, omu, dst_base):
        for (a0, a1) in MMR:
            n = a1 - a0
            kb.wait("sync", "VV")
            for ci in range(8):
                Ue = c.fa[:, ci * 520: ci * 520 + n + 1]
                if a0 == 0:
                    kb.dma("sync", Ue[:, 1:n + 1], u[ci * 128:(ci + 1) * 128, 0:n], "LD")
                    kb.dma("sync", Ue[:, 0:1], u[ci * 128:(ci + 1) * 128, 0:1], "LD", slow=True)
                else:
                    kb.dma("sync", Ue[:, 0:n + 1], u[ci * 128:(ci + 1) * 128, a0 - 1:a1], "LD")
            kb.wait("vector", "LD")
            for ci in range(8):
                Ue = c.fa[:, ci * 520: ci * 520 + n + 1]
                t2 = c.fb[:, ci * 520: ci * 520 + n]
                kb.op("vector", lambda e, Ue=Ue, t2=t2, ci=ci, n=n:
                      e.tensor_scalar_mul(out=t2, in0=Ue[:, 0:n], scalar1=pv(c.sm, omu, ci)))
            if a0 == 0:
                for ci in range(8):
                    t2 = c.fb[:, ci * 520: ci * 520 + n]
                    kb.op("vector", lambda e: e.drain())
                    kb.op("vector", lambda e, t2=t2:
                          e.tensor_scalar_mul(out=t2[:, 8:9], in0=t2[:, 8:9],
                                              scalar1=pv(c.sm, "flag")))
                kb.op("vector", lambda e: e.drain())
            for ci in range(8):
                Ue = c.fa[:, ci * 520: ci * 520 + n + 1]
                t2 = c.fb[:, ci * 520: ci * 520 + n]
                dst = c.ub[:, dst_base + ci * WTOK + a0: dst_base + ci * WTOK + a1]
                kb.op("vector", lambda e, Ue=Ue, t2=t2, d=dst, ci=ci, n=n, mu=mu:
                      e.scalar_tensor_tensor(out=d, in0=Ue[:, 1:n + 1],
                                             scalar=pv(c.sm, mu, ci), in1=t2,
                                             op0=ALU.mult, op1=ALU.add))
            kb.mark("vector", "VV")
        kb.wait("sync", "VV")
        kb.barrier()

    if upto < 2:
        return
    build_mix("mur", "omur", 0)
    if upto < 3:
        return
    evac_r = evac_to_dram(c, rr, func=AF.Sigmoid)
    emit_proj(c, 8, 8,
              lambda k, m: c.wa[:, k * 1024 + m * 128: k * 1024 + m * 128 + 128],
              lambda k, a0, a1: c.ub[:, k * WTOK + a0: k * WTOK + a1],
              evac_r)
    kb.barrier()
    if upto < 4:
        return
    build_mix("muk", "omuk", 0)
    if upto < 5:
        return
    for (a0, a1) in MMR:
        n = a1 - a0
        for m in range(8):
            ps = c.pp[m]
            for k in range(8):
                kb.op("tensor", lambda e, k=k, m=m, ps=ps, n=n, a0=a0, a1=a1:
                      e.matmul(out=ps[:, :n],
                               lhsT=c.wa[:, 8192 + k * 1024 + m * 128: 8192 + k * 1024 + m * 128 + 128],
                               rhs=c.ub[:, k * WTOK + a0: k * WTOK + a1],
                               start=(k == 0), stop=(k == 7)),
                      inc=("TT", 1) if k == 7 else None)
            ttv = kb.val("TT")
            kb.wait("scalar", "TT", ttv)
            kb.op("scalar", lambda e, ps=ps, n=n, m=m:
                  e.activation(out=c.fb[:, m * 520: m * 520 + n], in_=ps[:, :n], func=AF.Relu))
        kb.mark("scalar", "SS")
        kb.wait("scalar", "SS")
        for m in range(8):
            kb.op("scalar", lambda e, m=m, n=n, a0=a0, a1=a1:
                  e.activation(out=c.ub[:, m * WTOK + a0: m * WTOK + a1],
                               in_=c.fb[:, m * 520: m * 520 + n], func=AF.Square))
        kb.mark("scalar", "SS")
        kb.wait("tensor", "SS")
    kb.barrier()
    if upto < 6:
        return
    FS = [c.fa[:, i * 520: i * 520 + 512] for i in range(2)]
    fs_sg = {}
    _mmsel = {6: MMR[:1], 7: MMR[:2], 8: MMR[:4]}.get(upto, MMR)
    for (a0, a1) in _mmsel:
        n = a1 - a0
        kb.barrier()
        kb.wait("sync", "EV")
        for mo in range(8):
            kb.dma("sync", c.fb[:, mo * 520: mo * 520 + n],
                   x[mo * 128:(mo + 1) * 128, a0:a1], "LDX")
        ldx = kb.val("LDX")
        for mo in range(8):
            ps = c.pp[4 + mo % 2]
            if mo >= 2:
                kb.wait("tensor", "EV", kb.val("EV") - 1)
            for k in range(8):
                kb.op("tensor", lambda e, k=k, mo=mo, ps=ps, n=n, a0=a0, a1=a1:
                      e.matmul(out=ps[:, :n],
                               lhsT=c.wa[:, 16384 + k * 1024 + mo * 128: 16384 + k * 1024 + mo * 128 + 128],
                               rhs=c.ub[:, k * WTOK + a0: k * WTOK + a1],
                               start=(k == 0), stop=(k == 7)),
                      inc=("TT", 1) if k == 7 else None)
            ttv = kb.val("TT")
            fS = FS[mo % 2]
            rstage = c.fa[:, (2 + mo % 2) * 520: (2 + mo % 2) * 520 + n]
            if fs_sg.get(mo % 2):
                kb.wait("sync", "SG", fs_sg[mo % 2])
            ldr = kb.dma("sync", rstage, rr[mo * 128:(mo + 1) * 128, a0:a1], f"LDR{mo % 2}")
            kb.wait("vector", "TT", ttv)
            kb.wait("vector", f"LDR{mo % 2}", ldr)
            kb.wait("vector", "LDX", ldx)
            kb.op("vector", lambda e, fS=fS, ps=ps, rst=rstage, n=n:
                  e.tensor_tensor(out=fS[:, :n], in0=ps[:, :n], in1=rst, op=ALU.mult))
            kb.op("vector", lambda e: e.drain())
            kb.op("vector", lambda e, fS=fS, mo=mo, n=n:
                  e.tensor_add(out=fS[:, :n], in0=fS[:, :n],
                               in1=c.fb[:, mo * 520: mo * 520 + n]))
            kb.mark("vector", "EV")
            evv = kb.val("EV")
            kb.wait("gpsimd", "EV", evv)
            fs_sg[mo % 2] = kb.dma("gpsimd", x[mo * 128:(mo + 1) * 128, a0:a1],
                                   fS[:, :n], "SG")
    kb.barrier()



def emit_PC_qkv(c, upto=99):
    P, kb = c.P, c.P.kb
    emit_wload(c, "wqT", 0, 1024, 1024, 0)
    emit_wload(c, "wkTq", 0, 1024, 1024, 8192)
    emit_wload(c, "wvTq", 0, 1024, 1024, 16384)
    emit_ln(c, P.dram["x_scr"], ub_base=0)
    kb.barrier()
    if upto < 1:
        return
    q = P.dram["q_scr"]; kk = P.dram["k_scr"]; k8 = P.dram["k8_scr"]
    vv = P.dram["v_scr"]
    WS = [c.wstage[:, i * 520: i * 520 + 512] for i in range(4)]

    state = {"j": 0}; sg_hist = {}
    def evq(m, a0, a1, ps, ttv):
        n = a1 - a0
        j = state["j"]; state["j"] += 1
        si = j % 4; stage = WS[si]
        kb.wait("scalar", "TT", ttv)
        if si in sg_hist:
            kb.wait("scalar", f"SG{si}", sg_hist[si])
        kb.op("scalar", lambda e, s=stage, ps=ps, n=n, m=m:
              e.activation(out=s[:, :n], in_=ps[:, :n], func=AF.Identity,
                           bias=pv(c.sm, "bq", m)))
        kb.mark("scalar", "EV")
        ev = kb.val("EV")
        kb.wait("sync", "EV", ev)
        sg_hist[si] = kb.dma("sync", q[m * 128:(m + 1) * 128, a0 - HALO:a1 - HALO],
                             stage[:, :n], f"SG{si}")
        return ev
    emit_proj(c, 8, 8,
              lambda k, m: c.wa[:, k * 1024 + m * 128: k * 1024 + m * 128 + 128],
              lambda k, a0, a1: c.ub[:, k * WTOK + a0: k * WTOK + a1],
              evq, ranges=MMR_REAL)
    kb.barrier()
    if upto < 2:
        return

    state2 = {"j": 0}; sg2 = {}
    def evk(m, a0, a1, ps, ttv):
        n = a1 - a0
        j = state2["j"]; state2["j"] += 1
        si = j % 2
        stage = WS[si]; stage8 = WS[2 + si]
        kb.wait("scalar", "TT", ttv)
        if si in sg2:
            kb.wait("scalar", f"SG{2 + si}", sg2[si])
        kb.op("scalar", lambda e, s=stage, ps=ps, n=n, m=m:
              e.activation(out=s[:, :n], in_=ps[:, :n], func=AF.Identity,
                           bias=pv(c.sm, "bk", m)))
        kb.op("scalar", lambda e, s=stage8, st2=stage, n=n:
              e.activation(out=s[:, :n], in_=st2[:, :n], func=AF.Copy, scale=0.125))
        kb.mark("scalar", "EV")
        ev = kb.val("EV")
        kb.wait("sync", "EV", ev)
        kb.dma("sync", kk[m * 128:(m + 1) * 128, a0 - HALO:a1 - HALO], stage[:, :n], f"SG{2 + si}")
        sg2[si] = kb.dma("sync", k8[m * 128:(m + 1) * 128, a0 - HALO:a1 - HALO],
                         stage8[:, :n], f"SG{2 + si}")
        return ev
    emit_proj(c, 8, 8,
              lambda k, m: c.wa[:, 8192 + k * 1024 + m * 128: 8192 + k * 1024 + m * 128 + 128],
              lambda k, a0, a1: c.ub[:, k * WTOK + a0: k * WTOK + a1],
              evk, ranges=MMR_REAL)
    kb.barrier()

    if upto < 3:
        return
    kb.dma("sync", c.aux[0:1, 0:1024], P.dram["bvrow"][0:1, :], "LD")
    kb.wait("tensor", "LD")
    for half in range(2):
        kb.op("tensor", lambda e, half=half:
              e.matmul(out=c.pp[6][:, :512], lhsT=c.onesr[:, :],
                       rhs=c.aux[0:1, half * 512:(half + 1) * 512],
                       start=True, stop=True), inc=("TT", 1))
        kb.wait("scalar", "TT")
        kb.op("scalar", lambda e, half=half:
              e.activation(out=c.fb[:, half * 512:(half + 1) * 512],
                           in_=c.pp[6][:, :512], func=AF.Copy))
        kb.mark("scalar", "SS")
        kb.wait("tensor", "SS")
    kb.wait("vector", "SS")
    for si in range(2):
        kb.op("vector", lambda e, si=si:
              e.memset(bass.AP(c.wstage, si * 520 + 64, [[2080, 128], [65, 8], [1, 1]]), 1.0))
    kb.mark("vector", "VV")
    kb.barrier()
    vsg = {}
    vev = {}
    for tc in range(16):
        for nf in range(2):
            ps = c.pp[nf]
            psh = c.pp[nf]
            if nf in vev:
                kb.wait("tensor", "EV", vev[nf])
            for ci in range(8):
                kb.op("tensor", lambda e, ci=ci, tc=tc, nf=nf, ps=ps:
                      e.matmul(out=ps[:, :512],
                               lhsT=c.ub[:, ci * WTOK + HALO + tc * 128: ci * WTOK + HALO + (tc + 1) * 128],
                               rhs=c.wa[:, 16384 + ci * 1024 + nf * 512: 16384 + ci * 1024 + (nf + 1) * 512],
                               start=(ci == 0), stop=(ci == 7)),
                      inc=("TT", 1) if ci == 7 else None)
            ttv = kb.val("TT")
            si = nf
            kb.wait("vector", "TT", ttv)
            if si in vsg:
                kb.wait("vector", f"SG{4 + si}", vsg[si])
            out_ap = bass.AP(c.wstage, si * 520, [[2080, 128], [65, 8], [1, 64]])
            in0_ap = bass.AP(psh, 0, [[512, 128], [64, 8], [1, 64]])
            in1_ap = bass.AP(c.fb, nf * 512, [[5960, 128], [64, 8], [1, 64]])
            kb.op("vector", lambda e, o=out_ap, a=in0_ap, b=in1_ap:
                  e.tensor_tensor(out=o, in0=a, in1=b, op=ALU.add))
            kb.mark("vector", "EV")
            evv = kb.val("EV")
            vev[nf] = evv
            kb.wait("sync", "EV", evv)
            vsg[si] = kb.dma("sync",
                             P.dram["v_scr"][tc * 128:(tc + 1) * 128, nf * 520:(nf + 1) * 520],
                             c.wstage[:, si * 520: si * 520 + 520], f"SG{4 + si}")
    kb.barrier()


def emit_PC_bound(c, kf_rows):
    P, kb = c.P, c.P.kb
    qd, kf, px = P.dram["q_scr"], P.dram["kf_scr"], P.dram["pxT"]
    CEL = c.aux[0:1, 1536:1552]
    kb.op("vector", lambda e: e.memset(c.aux[0:1, 1536:1552], 0.0))
    kb.mark("vector", "VV")

    def maxnorm2(src, nrows_blocks, ncols, cell_off, pdim, isbf=False):
        for rb in range(nrows_blocks):
            for (a0, a1) in mm_ranges(0, ncols):
                n = a1 - a0
                kb.wait("sync", "SS")
                stagein = c.ub[:pdim, 0:n] if isbf else c.fa[:pdim, 0:n]
                kb.dma("sync", stagein, src[rb * pdim:(rb + 1) * pdim, a0:a1], "LD")
                kb.wait("scalar", "LD")
                kb.op("scalar", lambda e, n=n, pdim=pdim, si=stagein:
                      e.activation(out=c.fb[:pdim, 0:n], in_=si, func=AF.Square))
                kb.mark("scalar", "SS")
                kb.wait("tensor", "SS")
                kb.op("tensor", lambda e, n=n, pdim=pdim:
                      e.matmul(out=c.pp[6][0:1, :n], lhsT=c.onesc[:pdim, 0:1],
                               rhs=c.fb[:pdim, 0:n], start=True, stop=True),
                      inc=("TT", 1))
                kb.wait("vector", "TT")
                kb.op("vector", lambda e, n=n:
                      e.reduce_max(out=c.aux[0:1, 1555:1556], in_=c.pp[6][0:1, :n],
                                   axis=mybir.AxisListType.X))
                kb.op("vector", lambda e: e.drain())
                kb.op("vector", lambda e, co=cell_off:
                      e.tensor_tensor(out=c.aux[0:1, co:co + 1], in0=c.aux[0:1, co:co + 1],
                                      in1=c.aux[0:1, 1555:1556], op=ALU.max))
                kb.op("vector", lambda e: e.drain())
                kb.mark("vector", "VV")
                kb.wait("tensor", "VV")

    maxnorm2(qd, 8, NREAL, 1536, 128, isbf=True)
    maxnorm2(kf, 2 * kf_rows, NREAL, 1537, 128, isbf=True)
    maxnorm2(px, 1, JWC, 1538, 64)
    kb.wait("scalar", "VV")
    kb.op("scalar", lambda e: e.activation(out=c.aux[0:1, 1536:1539], in_=c.aux[0:1, 1536:1539],
                                           func=AF.Sqrt))
    kb.mark("scalar", "SS")
    kb.wait("vector", "SS")
    A_, M_ = ALU.add, ALU.mult
    kb.op("vector", lambda e: e.tensor_scalar(out=c.aux[0:1, 1540:1541], in0=c.aux[0:1, 1537:1538],
                                              scalar1=0.125, scalar2=None, op0=M_))
    kb.op("vector", lambda e: e.drain())
    kb.op("vector", lambda e: e.tensor_tensor(out=c.aux[0:1, 1540:1541], in0=c.aux[0:1, 1540:1541],
                                              in1=c.aux[0:1, 1538:1539], op=A_))
    kb.op("vector", lambda e: e.drain())
    kb.op("vector", lambda e: e.tensor_tensor(out=c.aux[0:1, 1540:1541], in0=c.aux[0:1, 1540:1541],
                                              in1=c.aux[0:1, 1536:1537], op=M_))
    kb.op("vector", lambda e: e.drain())
    kb.op("vector", lambda e: e.tensor_tensor(out=c.aux[0:1, 1541:1542], in0=c.aux[0:1, 1537:1538],
                                              in1=c.aux[0:1, 1538:1539], op=M_))
    kb.op("vector", lambda e: e.drain())
    kb.op("vector", lambda e: e.tensor_tensor(out=c.aux[0:1, 1540:1541], in0=c.aux[0:1, 1540:1541],
                                              in1=c.aux[0:1, 1541:1542], op=A_))
    kb.op("vector", lambda e: e.drain())
    kb.op("vector", lambda e: e.tensor_scalar_mul(out=c.aux[0:1, 1540:1541],
                                                  in0=c.aux[0:1, 1540:1541], scalar1=-1.0))
    kb.mark("vector", "VV")
    kb.wait("tensor", "VV")
    kb.op("tensor", lambda e: e.matmul(out=c.pp[6][:, 0:1], lhsT=c.onesr[:, :],
                                       rhs=c.aux[0:1, 1540:1541], start=True, stop=True),
          inc=("TT", 1))
    kb.wait("scalar", "TT")
    kb.op("scalar", lambda e: e.activation(out=c.aux[:, 1596:1597], in_=c.pp[6][:, 0:1],
                                           func=AF.Copy))
    kb.mark("scalar", "SS")
    kb.barrier()


def emit_PC_head(c, h):
    P, kb = c.P, c.P.kb
    kf, qd = P.dram["kf_scr"], P.dram["q_scr"]
    kp, qp = P.dram["kp_scr"], P.dram["qp_scr"]
    vv, x = P.dram["v_scr"], P.dram["x_scr"]
    k8f = P.dram["k8f_scr"]
    mh = c.aux[:, 1596:1597]
    LT = [c.wstage[:, 0:128], c.wstage[:, 520:648]]

    def window_pass(nchunks, win_w, src_lhs_fn, tab_fn, dst, dst_stride, jb_fn):
        lt_tt = {}
        ps_ev = {}
        as_sg = {}
        lt_ld = {}
        gslot = 0
        for ic in range(nchunks):
            jb = jb_fn(ic)
            sl = ic % 2
            if sl in lt_tt:
                kb.wait("sync", "TT", lt_tt[sl])
            lt_ld[sl] = kb.dma("sync", LT[sl][0:64, :], src_lhs_fn(ic), f"LT{sl}")
            for wi, (w0, w1) in enumerate(mm_ranges(0, win_w)):
                n = w1 - w0
                rsl = wi % 2
                ps = c.pp[rsl]
                if rsl in ps_ev:
                    kb.wait("tensor", "EV", ps_ev[rsl])
                kb.wait("tensor", f"LT{sl}", lt_ld[sl])
                kb.op("tensor", lambda e, lt=LT[sl], rt=tab_fn(jb + w0, jb + w1), ps=ps, n=n:
                      e.matmul(out=ps[:, :n], lhsT=lt[0:64, :], rhs=rt,
                               start=True, stop=True), inc=("TT", 1))
                ttv = kb.val("TT")
                lt_tt[sl] = ttv
                slot = gslot % 4
                gslot += 1
                kb.wait("scalar", "TT", ttv)
                if slot in as_sg:
                    kb.wait("scalar", f"SG{slot}", as_sg[slot])
                kb.op("scalar", lambda e, ps=ps, n=n, slot=slot:
                      e.activation(out=c.astage[:, slot * 520: slot * 520 + n],
                                   in_=ps[:, :n], func=AF.Copy), inc=("EV", 1))
                ps_ev[rsl] = kb.val("EV")
                stq = "sync" if slot % 2 == 0 else "gpsimd"
                kb.wait(stq, "EV", ps_ev[rsl])
                as_sg[slot] = kb.dma(stq, dst[ic * 128:(ic + 1) * 128, w0:w1],
                                     c.astage[:, slot * 520: slot * 520 + n], f"SG{slot}")

    window_pass(
        32, 2176,
        lambda kc: kf[(kc * 128 // 2048) * 1024 + h * 64: (kc * 128 // 2048) * 1024 + h * 64 + 64,
                      (kc * 128) % 2048: (kc * 128) % 2048 + 128],
        lambda j0, j1: c.ub[0:64, UBPX + j0: UBPX + j1], kp, 2176,
        lambda kc: 3968 - kc * 128)
    kb.barrier()
    window_pass(
        16, 4223,
        lambda qc: qd[h * 64: h * 64 + 64, qc * 128:(qc + 1) * 128],
        lambda j0, j1: c.rstage[0:64, 1 + j0: 1 + j1], qp, 4224,
        lambda qc: 1921 - qc * 128)
    kb.barrier()
    for kc in range(32):
        kb.dma("gpsimd", c.astage[:, 2080 + kc * 65: 2080 + kc * 65 + 65],
               vv[kc * 128:(kc + 1) * 128, h * 65: h * 65 + 65], "LV0")
    v_ld = kb.val("LV0")

    qtile_tt = 0
    qldv = 0
    for qt in range(4):
        q0 = qt * 512
        kb.wait("sync", "TT", qtile_tt)
        qldv = kb.dma("sync", c.ub[0:64, 0:512], qd[h * 64: h * 64 + 64, q0:q0 + 512], "LQ0")
        kb.wait("tensor", "EV")
        st_tt = {}
        sev = {}
        for kc in range(32):
            k0 = kc * 128
            rank, col0 = k0 // 2048, k0 % 2048
            sl = kc % 2
            ps = c.pp[sl]
            if sl in st_tt:
                kb.wait("sync", "TT", st_tt[sl])
                kb.wait("gpsimd", "TT", st_tt[sl])
            ltv = kb.dma("sync", LT[sl][0:64, :],
                         k8f[rank * 1024 + h * 64: rank * 1024 + h * 64 + 64, col0:col0 + 128],
                         f"LT{sl}")
            dqb = c.ub[:, 1040 + sl * 1040: 1040 + sl * 1040 + 512]
            for jb in range(4):
                qch = qt * 4 + jb
                off = qch * 128 * 4224 + (k0 + 127)
                kb.dma("gpsimd", dqb[:, jb * 128:(jb + 1) * 128],
                       bass.AP(qp.tensor, off, [[4224 - 1, 128], [1, 128]]), f"LR{sl}")
            dqv = kb.val(f"LR{sl}")
            dkt = c.ub[:, 3640 + sl * 520: 3640 + sl * 520 + 512]
            off2 = k0 * 2176 + q0 + 127
            dkv = kb.dma("sync", dkt, bass.AP(kp.tensor, off2, [[2176 - 1, 128], [1, 512]]),
                         f"LK{sl}")
            if sl in sev:
                kb.wait("tensor", "EV", sev[sl])
            kb.wait("tensor", f"LT{sl}", ltv)
            kb.wait("tensor", f"LR{sl}", dqv)
            kb.wait("tensor", f"LK{sl}", dkv)
            kb.wait("tensor", "LQ0", qldv)
            kb.op("tensor", lambda e, lt=LT[sl], ps=ps:
                  e.matmul(out=ps[:, :512], lhsT=lt[0:64, :], rhs=c.ub[0:64, 0:512],
                           start=True, stop=False))
            for jb in range(4):
                kb.op("tensor", lambda e, dqb=dqb, jb=jb, ps=ps:
                      e.matmul(out=ps[:, jb * 128:(jb + 1) * 128],
                               lhsT=dqb[:, jb * 128:(jb + 1) * 128], rhs=c.eyeb[:, :],
                               start=False, stop=False))
            kb.op("tensor", lambda e, dkt=dkt, ps=ps:
                  e.matmul(out=ps[:, :512], lhsT=c.eyeb[:, :], rhs=dkt,
                           start=False, stop=True), inc=("TT", 1))
            ttv = kb.val("TT")
            st_tt[sl] = ttv
            kb.wait("scalar", "TT", ttv)
            kb.op("scalar", lambda e, ps=ps, kc=kc:
                  e.activation(out=c.wa[:, kc * 512:(kc + 1) * 512], in_=ps[:, :512],
                               func=AF.Exp, bias=mh), inc=("EV", 1))
            sev[sl] = kb.val("EV")
        kb.wait("tensor", "EV")
        kb.wait("tensor", "LV0", v_ld)
        psv = c.pp[4]
        for kc in range(32):
            kb.op("tensor", lambda e, kc=kc, psv=psv:
                  e.matmul(out=psv[0:65, :512],
                           lhsT=c.astage[:, 2080 + kc * 65: 2080 + kc * 65 + 65],
                           rhs=c.wa[:, kc * 512:(kc + 1) * 512],
                           start=(kc == 0), stop=(kc == 31)),
                  inc=("TT", 1) if kc == 31 else None)
        ttv = kb.val("TT")
        qtile_tt = ttv
        kb.wait("vector", "TT", ttv)
        kb.op("vector", lambda e, psv=psv:
              e.reciprocal(out=c.aux[0:1, 0:512], in_=psv[64:65, :512]))
        kb.mark("vector", "VV")
        kb.wait("tensor", "VV")
        kb.op("tensor", lambda e: e.matmul(out=c.pp[6][0:64, :512], lhsT=c.onesr[0:1, 0:64],
                                           rhs=c.aux[0:1, 0:512], start=True, stop=True),
              inc=("TT", 1))
        kb.wait("scalar", "TT")
        kb.op("scalar", lambda e: e.activation(out=c.fb[0:64, 0:512], in_=c.pp[6][0:64, :512],
                                               func=AF.Copy))
        kb.mark("scalar", "SS2")
        ch, pb = h // 2, (h % 2) * 64
        kb.wait("sync", "EV")
        kb.wait("sync", "SG")
        ldx = kb.dma("sync", c.fa[0:64, 0:512], x[ch * 128 + pb: ch * 128 + pb + 64,
                                                  HALO + q0: HALO + q0 + 512], "LDX")
        kb.wait("vector", "SS2")
        kb.op("vector", lambda e, psv=psv:
              e.tensor_tensor(out=c.fb[0:64, 520:1032], in0=psv[0:64, :512],
                              in1=c.fb[0:64, 0:512], op=ALU.mult))
        kb.wait("vector", "LDX", ldx)
        kb.op("vector", lambda e: e.drain())
        kb.op("vector", lambda e:
              e.tensor_add(out=c.fa[0:64, 0:512], in0=c.fa[0:64, 0:512],
                           in1=c.fb[0:64, 520:1032]))
        kb.mark("vector", "EV")
        kb.wait("sync", "EV")
        kb.dma("sync", x[ch * 128 + pb: ch * 128 + pb + 64, HALO + q0: HALO + q0 + 512],
               c.fa[0:64, 0:512], "SG")
        kb.wait("sync", "SG")
    kb.barrier()


def emit_PC_prep(c):
    P, kb = c.P, c.P.kb
    for ti, src in enumerate(("pxT", "pxrT")):
        s = P.dram[src]
        for (a0, a1) in mm_ranges(0, JWC):
            n = a1 - a0
            kb.wait("sync", "VV")
            kb.dma("sync", c.fa[0:64, 0:n], s[0:64, a0:a1], "LD")
            kb.wait("vector", "LD")
            dst = (c.ub[0:64, UBPX + a0: UBPX + a1] if ti == 0
                   else c.rstage[0:64, 1 + a0: 1 + a1])
            kb.op("vector", lambda e, n=n, dst=dst:
                  e.tensor_copy(out=dst, in_=c.fa[0:64, 0:n]))
            kb.mark("vector", "VV")
    kb.barrier()



def emit_PD_route(c, upto=99):
    P, kb = c.P, c.P.kb
    u, x = P.dram["u_scr"], P.dram["x_scr"]
    comb = P.dram["comb_scr"]
    emit_ln(c, x, ub_base=0, out_f32_dram=u)
    kb.barrier()
    for ci in range(8):
        kb.dma("sync", c.fa[:, 4800 + ci * 16: 4800 + (ci + 1) * 16],
               P.dram["gwT"][ci * 128:(ci + 1) * 128, :], "LD")
    kb.dma("sync", c.aux[:, 120:136], P.dram["ebbt"][:, :], "LD")
    kb.op("scalar", lambda e: e.nop())
    kb.mark("scalar", "SS")
    kb.dma("sync", c.fb[:, 5404:5660], P.dram["ltmask"][:, :], "LD")
    kb.wait("vector", "LD")
    kb.wait("vector", "SS")
    ZER = c.fb[:, 5660:5676]
    kb.op("vector", lambda e: e.memset(ZER, 0.0))
    kb.mark("vector", "VV")
    kb.barrier()

    if upto < 1:
        return
    for qt in range(4):
        kb.wait("sync", "TT")
        for ci in range(8):
            kb.dma("sync", c.fa[:, ci * 520: ci * 520 + 512],
                   u[ci * 128:(ci + 1) * 128, HALO + qt * 512: HALO + (qt + 1) * 512], "LD")
        ldv = kb.val("LD")
        kb.wait("tensor", "LD", ldv)
        kb.wait("tensor", "VV")
        kb.wait("tensor", "SS")
        for ci in range(8):
            kb.op("tensor", lambda e, ci=ci:
                  e.matmul(out=c.pp[6][0:16, :512], lhsT=c.fa[:, 4800 + ci * 16: 4800 + (ci + 1) * 16],
                           rhs=c.fa[:, ci * 520: ci * 520 + 512],
                           start=(ci == 0), stop=(ci == 7)),
                  inc=("TT", 1) if ci == 7 else None)
        kb.wait("scalar", "TT")
        if "dbglg" in P.dram:
            kb.op("scalar", lambda e: e.activation(out=c.fb[0:16, 3200:3712], in_=c.pp[6][0:16, :512],
                                                   func=AF.Copy))
            kb.mark("scalar", "SS")
            kb.wait("sync", "SS")
            kb.dma("sync", P.dram["dbglg"][0:16, qt * 512:(qt + 1) * 512], c.fb[0:16, 3200:3712], "SG")
        kb.op("scalar", lambda e: e.activation(out=c.fb[0:16, 0:512], in_=c.pp[6][0:16, :512],
                                               func=AF.Sigmoid))
        kb.mark("scalar", "SS")
        kb.wait("tensor", "SS")
        for tc in range(4):
            kb.op("tensor", lambda e, tc=tc:
                  e.transpose(out=c.pp[7][:, tc * 16:(tc + 1) * 16],
                              in_=c.fb[0:16, tc * 128:(tc + 1) * 128],
                              identity=c.eyef_sb[0:16, 0:16]),
                  inc=("TT", 1))
        kb.wait("vector", "TT")
        kb.op("vector", lambda e: e.tensor_copy(out=c.fb[:, 1040:1104], in_=c.pp[7][:, 0:64]))
        kb.op("vector", lambda e: e.drain())
        if "dbgstm" in P.dram:
            kb.wait("sync", "SS")
            for tc in range(4):
                kb.dma("sync", P.dram["dbgstm"][qt * 512 + tc * 128: qt * 512 + (tc + 1) * 128, :],
                       c.fb[:, 1040 + tc * 16: 1040 + (tc + 1) * 16], "SG")
        if upto < 2:
            continue
        for tc in range(4):
            S_tm = c.fb[:, 1040 + tc * 16: 1040 + (tc + 1) * 16]
            CH_ = c.fb[:, 1200 + tc * 16: 1200 + (tc + 1) * 16]
            G = c.fb[:, 1600:1604]; MSK = c.fb[:, 1680:1684]
            TMP = c.fb[:, 1700:1716]
            B1 = c.fb[:, 1720:1976]; B2 = c.fb[:, 1980:2236]
            AA = c.fb[:, 2240:2496]
            RNK = c.fb[:, 2500:2516]; SEL = c.fb[:, 2520:2536]
            WW = c.fb[:, 2540:2556]; DEN = c.fb[:, 2560:2561]
            CMB = c.fb[:, 2580:2596]
            RK = c.fb[:, 1660:1676]
            def dve(f):
                kb.op("vector", f)
            def dr():
                kb.op("vector", lambda e: e.drain())
            dve(lambda e, o=CH_, a=S_tm: e.tensor_add(out=o, in0=a, in1=c.aux[:, 120:136])); dr()
            if upto < 3:
                continue
            first = True
            for (a, b) in ((0, 1), (0, 2), (0, 3), (1, 2), (1, 3), (2, 3)):
                pa = bass.AP(c.fb, 1200 + tc * 16 + a, [[5960, 128], [4, 4]])
                pb2 = bass.AP(c.fb, 1200 + tc * 16 + b, [[5960, 128], [4, 4]])
                t = c.fb[:, 1604:1608]
                dve(lambda e, pa=pa, pb2=pb2, t=t: e.tensor_tensor(out=t, in0=pa, in1=pb2, op=ALU.add)); dr()
                if first:
                    dve(lambda e, t=t, G=G: e.tensor_copy(out=G, in_=t)); first = False
                else:
                    dve(lambda e, t=t, G=G: e.tensor_tensor(out=G, in0=G, in1=t, op=ALU.max))
                dr()
            for g in range(4):
                dve(lambda e, g=g, G=G: e.tensor_copy(out=c.fb[:, 1620 + g * 4:1624 + g * 4], in_=G)); dr()
                dve(lambda e, g=g: e.tensor_scalar(out=c.fb[:, 1640 + g * 4:1644 + g * 4],
                                                   in0=c.fb[:, 5660:5664],
                                                   scalar1=c.fb[:, 1600 + g:1601 + g],
                                                   scalar2=None, op0=ALU.add)); dr()
            dve(lambda e: e.tensor_tensor(out=c.fb[:, 1620:1636], in0=c.fb[:, 1620:1636],
                                          in1=c.fb[:, 1640:1656], op=ALU.is_gt)); dr()
            dve(lambda e, RK=RK: e.tensor_tensor(out=RK[:, 0:4],
                                          in0=bass.AP(c.fb, 1620, [[5960, 128], [4, 4]]),
                                          in1=bass.AP(c.fb, 1621, [[5960, 128], [4, 4]]), op=ALU.add)); dr()
            dve(lambda e, RK=RK: e.tensor_tensor(out=RK[:, 4:8],
                                          in0=bass.AP(c.fb, 1622, [[5960, 128], [4, 4]]),
                                          in1=bass.AP(c.fb, 1623, [[5960, 128], [4, 4]]), op=ALU.add)); dr()
            dve(lambda e, RK=RK: e.tensor_tensor(out=RK[:, 0:4], in0=RK[:, 0:4], in1=RK[:, 4:8], op=ALU.add)); dr()
            dve(lambda e, RK=RK, MSK=MSK: e.tensor_single_scalar(out=MSK, in_=RK[:, 0:4], scalar=2.0, op=ALU.is_lt)); dr()
            if upto < 4:
                continue
            for a in range(4):
                pa = bass.AP(c.fb, 1200 + tc * 16 + a, [[5960, 128], [4, 4]])
                po = bass.AP(c.fb, 1700 + a, [[5960, 128], [4, 4]])
                dve(lambda e, pa=pa, po=po, MSK=MSK: e.tensor_tensor(out=po, in0=pa, in1=MSK, op=ALU.mult)); dr()
            for e_ in range(16):
                dve(lambda e, e_=e_, TMP=TMP: e.tensor_copy(out=c.fb[:, 1720 + e_ * 16:1736 + e_ * 16], in_=TMP)); dr()
                dve(lambda e, e_=e_: e.tensor_scalar(out=c.fb[:, 1980 + e_ * 16:1996 + e_ * 16],
                                                     in0=c.fb[:, 5660:5676],
                                                     scalar1=c.fb[:, 1700 + e_:1701 + e_],
                                                     scalar2=None, op0=ALU.add)); dr()
            if upto < 5:
                continue
            dve(lambda e, AA=AA, B1=B1, B2=B2: e.tensor_tensor(out=AA, in0=B1, in1=B2, op=ALU.is_gt)); dr()
            dve(lambda e, B1=B1, B2=B2: e.tensor_tensor(out=B1, in0=B1, in1=B2, op=ALU.is_equal)); dr()
            dve(lambda e, B1=B1: e.tensor_tensor(out=B1, in0=B1, in1=c.fb[:, 5404:5660], op=ALU.mult)); dr()
            dve(lambda e, AA=AA, B1=B1: e.tensor_tensor(out=AA, in0=AA, in1=B1, op=ALU.add)); dr()
            dve(lambda e, RNK=RNK: e.tensor_copy(out=RNK, in_=bass.AP(c.fb, 2240, [[5960, 128], [16, 16]]))); dr()
            for j in range(1, 16):
                dve(lambda e, j=j, RNK=RNK: e.tensor_tensor(out=RNK, in0=RNK,
                                                   in1=bass.AP(c.fb, 2240 + j, [[5960, 128], [16, 16]]),
                                                   op=ALU.add)); dr()
            dve(lambda e, RNK=RNK, SEL=SEL: e.tensor_single_scalar(out=SEL, in_=RNK, scalar=4.0, op=ALU.is_lt)); dr()
            dve(lambda e, SEL=SEL, WW=WW, S_tm=S_tm: e.tensor_tensor(out=WW, in0=SEL, in1=S_tm, op=ALU.mult)); dr()
            dve(lambda e, WW=WW, DEN=DEN: e.reduce_sum(out=DEN, in_=WW, axis=mybir.AxisListType.X)); dr()
            dve(lambda e, DEN=DEN: e.tensor_scalar_add(out=DEN, in0=DEN, scalar1=1e-20)); dr()
            dve(lambda e, DEN=DEN: e.reciprocal(out=DEN, in_=DEN)); dr()
            dve(lambda e, WW=WW, DEN=DEN, CMB=CMB: e.tensor_scalar(out=CMB, in0=WW, scalar1=DEN, scalar2=None, op0=ALU.mult)); dr()
            kb.mark("vector", "VV")
            if upto < 6:
                continue
            kb.wait("tensor", "VV")
            kb.op("tensor", lambda e, CMB=CMB:
                  e.transpose(out=c.pp[5][0:16, 0:128], in_=CMB[:, 0:16],
                              identity=c.eyef_sb[:, 0:128]),
                  inc=("TT", 1))
            kb.wait("scalar", "TT")
            kb.op("scalar", lambda e: e.activation(out=c.fb[0:16, 3000:3128],
                                                   in_=c.pp[5][0:16, 0:128], func=AF.Copy))
            kb.mark("scalar", "SS")
            kb.wait("gpsimd", "SS")
            kb.wait("gpsimd", "SG")
            kb.dma("gpsimd", comb[0:16, qt * 512 + tc * 128: qt * 512 + (tc + 1) * 128],
                   c.fb[0:16, 3000:3128], "SG")
            kb.wait("vector", "SG")
    kb.barrier()


def emit_PD_experts(c, nexp=16, exs=None):
    P, kb = c.P, c.P.kb
    comb, out = P.dram["comb_scr"], P.dram["out_scr"]
    x = P.dram["x_scr"]
    for ci in range(8):
        kb.dma("sync", out[ci * 128:(ci + 1) * 128, :], x[ci * 128:(ci + 1) * 128, HALO:WTOK], "LD")
    kb.wait("sync", "LD")
    kb.barrier()
    yslot = {}
    stage_sg = {}
    for ex in (exs if exs is not None else range(nexp)):
        emit_wload(c, "gpT", ex * 1024, 1024, 512, 0)
        emit_wload(c, "upT", ex * 1024, 1024, 512, 4096)
        emit_wload(c, "dpT", ex * 512, 512, 1024, 8192)
        kb.barrier()
        for qt in range(4):
            kb.wait("sync", "TT")
            ldc = kb.dma("sync", c.aux[0:1, 0:512], comb[ex:ex + 1, qt * 512:(qt + 1) * 512], "LDC")
            kb.wait("tensor", "LDC", ldc)
            kb.wait("tensor", "EV")
            kb.wait("tensor", "SS")
            kb.wait("tensor", "EV2")
            kb.wait("vector", "TT")
            kb.op("tensor", lambda e: e.matmul(out=c.pp[6][:, :512], lhsT=c.onesr[:, :],
                                               rhs=c.aux[0:1, 0:512], start=True, stop=True),
                  inc=("TT", 1))
            kb.wait("scalar", "TT")
            kb.op("scalar", lambda e: e.activation(out=c.fb[:, 4160:4672], in_=c.pp[6][:, :512],
                                                   func=AF.Copy))
            kb.mark("scalar", "SS")
            kb.wait("vector", "SS")
            for io in range(4):
                psG, psU = c.pp[io % 2], c.pp[2 + io % 2]
                for ciw in range(8):
                    kb.op("tensor", lambda e, ciw=ciw, io=io, qt=qt, ps=psG:
                          e.matmul(out=ps[:, :512],
                                   lhsT=c.wa[:, ciw * 512 + io * 128: ciw * 512 + (io + 1) * 128],
                                   rhs=c.ub[:, ciw * WTOK + HALO + qt * 512: ciw * WTOK + HALO + (qt + 1) * 512],
                                   start=(ciw == 0), stop=(ciw == 7)))
                for ciw in range(8):
                    kb.op("tensor", lambda e, ciw=ciw, io=io, qt=qt, ps=psU:
                          e.matmul(out=ps[:, :512],
                                   lhsT=c.wa[:, 4096 + ciw * 512 + io * 128: 4096 + ciw * 512 + (io + 1) * 128],
                                   rhs=c.ub[:, ciw * WTOK + HALO + qt * 512: ciw * WTOK + HALO + (qt + 1) * 512],
                                   start=(ciw == 0), stop=(ciw == 7)),
                          inc=("TT", 1) if ciw == 7 else None)
                ttv = kb.val("TT")
                kb.wait("scalar", "TT", ttv)
                kb.op("scalar", lambda e, psG=psG, io=io:
                      e.activation(out=c.fb[:, (io % 2) * 520: (io % 2) * 520 + 512],
                                   in_=psG[:, :512], func=AF.Silu))
                kb.mark("scalar", "SS2")
                kb.wait("vector", "SS2")
                kb.op("vector", lambda e, psU=psU, io=io:
                      e.tensor_tensor(out=c.fa[:, (io % 2) * 520: (io % 2) * 520 + 512],
                                      in0=psU[:, :512],
                                      in1=c.fb[:, (io % 2) * 520: (io % 2) * 520 + 512],
                                      op=ALU.mult))
                kb.op("vector", lambda e: e.drain())
                kb.op("vector", lambda e, io=io:
                      e.tensor_tensor(out=c.wstage[:, io * 520: io * 520 + 512],
                                      in0=c.fa[:, (io % 2) * 520: (io % 2) * 520 + 512],
                                      in1=c.fb[:, 4160:4672], op=ALU.mult))
                kb.mark("vector", "EV")
                kb.wait("tensor", "EV")
            for fo in range(8):
                ps = c.pp[4 + fo % 2]
                if fo >= 2:
                    kb.wait("tensor", "EV2", kb.val("EV2") - 1)
                for i4 in range(4):
                    kb.op("tensor", lambda e, i4=i4, fo=fo, ps=ps:
                          e.matmul(out=ps[:, :512],
                                   lhsT=c.wa[:, 8192 + i4 * 1024 + fo * 128: 8192 + i4 * 1024 + (fo + 1) * 128],
                                   rhs=c.wstage[:, i4 * 520: i4 * 520 + 512],
                                   start=(i4 == 0), stop=(i4 == 3)),
                          inc=("TT", 1) if i4 == 3 else None)
                ttv = kb.val("TT")
                slot = (fo, qt)
                if slot in yslot:
                    kb.wait("sync", f"SG{2 + fo % 2}", yslot[slot])
                if (fo % 2) in stage_sg:
                    kb.wait("sync", f"SG{2 + fo % 2}", stage_sg[fo % 2])
                ldv = kb.dma("sync", c.fa[:, (2 + fo % 2) * 520: (2 + fo % 2) * 520 + 512],
                             out[fo * 128:(fo + 1) * 128, qt * 512:(qt + 1) * 512], f"LDY{fo % 2}")
                kb.wait("vector", "TT", ttv)
                kb.wait("vector", f"LDY{fo % 2}", ldv)
                kb.op("vector", lambda e, ps=ps, fo=fo:
                      e.tensor_tensor(out=c.fa[:, (2 + fo % 2) * 520: (2 + fo % 2) * 520 + 512],
                                      in0=ps[:, :512],
                                      in1=c.fa[:, (2 + fo % 2) * 520: (2 + fo % 2) * 520 + 512],
                                      op=ALU.add))
                kb.mark("vector", "EV2")
                ev2 = kb.val("EV2")
                kb.wait("gpsimd", "EV2", ev2)
                yslot[slot] = stage_sg[fo % 2] = kb.dma(
                    "gpsimd", out[fo * 128:(fo + 1) * 128, qt * 512:(qt + 1) * 512],
                    c.fa[:, (2 + fo % 2) * 520: (2 + fo % 2) * 520 + 512], f"SG{2 + fo % 2}")
    kb.wait("sync", "SG")
    kb.barrier()



def _copy_in(P, kb, dst_name, src_name, rows):
    for ci in range(rows // 128):
        kb.dma("sync", P.dram[dst_name][ci * 128:(ci + 1) * 128, :],
               P.dram[src_name][ci * 128:(ci + 1) * 128, :], "LD")
    kb.wait("sync", "LD")
    kb.barrier()


def build_launch_A():
    P = Prog()
    P.din("pvec", [128, PV_COLS]); P.din("eyef", [128, 128])
    P.din("x0", [1024, WTOK])
    P.din("aw", [4096, 3072]); P.din("apw", [4096, 1024])
    P.din("fcw", [4096, 4096]); P.din("mpw", [4 * 4096, 1024])
    P.din("Cm", [1024, 1024]); P.din("Sm", [1024, 1024])
    P.din("frt", [1024, WTOK]); P.din("fit", [1024, WTOK])
    P.din("wrT", [1024, 1024]); P.din("wkT", [1024, 1024]); P.din("wvT", [1024, 1024])
    P.din("wqT", [1024, 1024]); P.din("wkTq", [1024, 1024]); P.din("wvTq", [1024, 1024])
    P.din("bvrow", [1, 1024])
    P.dscr("qkv_scr", [3072, WTOK])
    P.dscr("wri_scr", [2048, WTOK], BF16)
    P.dscr("h_scr", [4096, WTOK], BF16)
    P.dscr("y_scr", [1024, WTOK])
    P.dscr("u_scr", [1024, WTOK]); P.dscr("r_scr", [1024, WTOK])
    P.dout("x_scr", [1024, WTOK])
    P.dout("q_scr", [1024, 2048], BF16)
    P.dout("k_scr", [1024, 2048], BF16)
    P.dout("k8_scr", [1024, 2048], BF16)
    P.dram["v_scr"] = P.nc.dram_tensor("vloc_scr", [2048, 1040], BF16,
                                       kind="ExternalOutput").ap()
    c = setup_common(P)
    kb = P.kb
    _copy_in(P, kb, "x_scr", "x0", 1024)
    emit_ln(c, P.dram["x_scr"], out_f32_dram=P.dram["y_scr"])
    kb.barrier()
    import os as _os
    for li in range(int(_os.environ.get("FNL", "4"))):
        emit_A1(c, li)
        emit_A2(c, li)
        emit_A3(c, li)
        emit_A4(c, li)
        emit_A5(c, li)
    emit_residual_xy(c, "x_scr", "x_scr", "y_scr")
    emit_PB(c)
    emit_PC_qkv(c)
    kb.barrier()
    P.kb.finalize(P.stack)
    return P


def build_launch_B():
    P = Prog()
    P.din("pvec", [128, PV_COLS]); P.din("eyef", [128, 128])
    P.din("xin", [1024, WTOK])
    P.din("q_scr", [1024, 2048], BF16)
    P.din("kf_scr", [2048, 2048], BF16)
    P.din("k8f_scr", [2048, 2048], BF16)
    P.dram["v_scr"] = P.nc.dram_tensor("v_full", [4096, 1040], BF16,
                                       kind="ExternalInput").ap()
    P.din("pxT", [64, JWC]); P.din("pxrT", [64, JWC])
    P.din("gwT", [1024, 16]); P.din("ebbt", [128, 16]); P.din("ltmask", [128, 256])
    P.din("gpT", [16 * 1024, 512]); P.din("upT", [16 * 1024, 512])
    P.din("dpT", [16 * 512, 1024])
    P.dscr("pxb_scr", [64, JWC], BF16); P.dscr("pxrb_scr", [64, JWC], BF16)
    P.dscr("kp_scr", [4096, 2176], BF16); P.dscr("qp_scr", [2048, 4224], BF16)
    P.dscr("x_scr", [1024, WTOK])
    P.dscr("u_scr", [1024, WTOK]); P.dscr("comb_scr", [16, 2048])
    P.dout("out_scr", [1024, 2048])
    c = setup_common(P)
    kb = P.kb
    _copy_in(P, kb, "x_scr", "xin", 1024)
    emit_PC_prep(c)
    emit_PC_bound(c, 1)
    for h in range(H):
        emit_PC_head(c, h)
    emit_PD_route(c)
    emit_PD_experts(c)
    kb.barrier()
    P.kb.finalize(P.stack)
    return P


def emit_exchange(c):
    P, kb = c.P, c.P.kb
    kb.barrier()
    groups = [[0, 1], [2, 3], [4, 5], [6, 7]]
    kb.sem("CC")
    h = kb.handles
    def cc(e, ins, outs):
        return e.collective_compute("AllGather", ALU.bypass, groups,
                                    ins=[ins], outs=[outs])
    v1 = kb.bump("CC", 1)
    kb.add("gpsimd", lambda e: cc(e, P.dram["k_scr"].opt(), P.dram["kf_scr"].opt())
           .then_inc(h["CC"]))
    v2 = kb.bump("CC", 1)
    kb.add("gpsimd", lambda e: cc(e, P.dram["k8_scr"].opt(), P.dram["k8f_scr"].opt())
           .then_inc(h["CC"]))
    v3 = kb.bump("CC", 1)
    kb.add("gpsimd", lambda e: cc(e, P.dram["vloc_scr"].opt(), P.dram["v_scr"].opt())
           .then_inc(h["CC"]))
    kb.wait("gpsimd", "CC", v3)
    kb.barrier()


def build_full(upto=99):
    P = Prog()
    P.din("pvec", [128, PV_COLS]); P.din("eyef", [128, 128])
    P.din("x0", [1024, WTOK])
    P.din("aw", [4096, 3072]); P.din("apw", [4096, 1024])
    P.din("fcw", [4096, 4096]); P.din("mpw", [4 * 4096, 1024])
    P.din("Cm", [1024, 1024]); P.din("Sm", [1024, 1024])
    P.din("frt", [1024, WTOK]); P.din("fit", [1024, WTOK])
    P.din("wrT", [1024, 1024]); P.din("wkT", [1024, 1024]); P.din("wvT", [1024, 1024])
    P.din("wqT", [1024, 1024]); P.din("wkTq", [1024, 1024]); P.din("wvTq", [1024, 1024])
    P.din("bvrow", [1, 1024])
    P.din("pxT", [64, JWC]); P.din("pxrT", [64, JWC])
    P.din("gwT", [1024, 16]); P.din("ebbt", [128, 16]); P.din("ltmask", [128, 256])
    P.din("gpT", [16 * 1024, 512]); P.din("upT", [16 * 1024, 512])
    P.din("dpT", [16 * 512, 1024])
    P.dscr("qkv_scr", [3072, WTOK])
    P.dscr("wri_scr", [2048, WTOK], BF16)
    P.dscr("h_scr", [4096, WTOK], BF16)
    P.dscr("y_scr", [1024, WTOK])
    P.dscr("u_scr", [1024, WTOK]); P.dscr("r_scr", [1024, WTOK])
    P.dscr("x_scr", [1024, WTOK])
    P.dscr("q_scr", [1024, 2048], BF16)
    P.dscr("k_scr", [1024, 2048], BF16)
    P.dscr("k8_scr", [1024, 2048], BF16)
    P.dram["vloc_scr"] = P.nc.dram_tensor("vloc_scr", [2048, 1040], BF16,
                                          kind="Internal").ap()
    P.dscr("kf_scr", [2048, 2048], BF16)
    P.dscr("k8f_scr", [2048, 2048], BF16)
    P.dram["v_scr"] = P.nc.dram_tensor("v_full", [4096, 1040], BF16,
                                       kind="Internal").ap()
    P.dscr("pxb_scr", [64, JWC], BF16); P.dscr("pxrb_scr", [64, JWC], BF16)
    P.dscr("kp_scr", [4096, 2176], BF16); P.dscr("qp_scr", [2048, 4224], BF16)
    P.dscr("comb_scr", [16, 2048])
    P.dout("out_scr", [1024, 2048])
    c = setup_common(P)
    kb = P.kb
    _copy_in(P, kb, "x_scr", "x0", 1024)
    emit_ln(c, P.dram["x_scr"], out_f32_dram=P.dram["y_scr"])
    kb.barrier()
    import os as _os
    for li in range(int(_os.environ.get("FNL", "4"))):
        emit_A1(c, li)
        emit_A2(c, li)
        emit_A3(c, li)
        emit_A4(c, li)
        emit_A5(c, li)
    emit_residual_xy(c, "x_scr", "x_scr", "y_scr")
    if upto >= 1:
        emit_PB(c)
    if upto >= 2:
        _vs = P.dram["v_scr"]
        P.dram["v_scr"] = P.dram["vloc_scr"]
        emit_PC_qkv(c)
        P.dram["v_scr"] = _vs
    if upto >= 3:
        emit_exchange(c)
    if upto >= 4:
        emit_PC_prep(c)
        emit_PC_bound(c, 1)
    if upto >= 5:
        for h in range(H):
            emit_PC_head(c, h)
    if upto >= 6:
        emit_PD_route(c)
        emit_PD_experts(c)
    kb.barrier()
    P.kb.finalize(P.stack)
    return P


def _stage_all(inp):
    C, Sneg = make_dft_consts()
    fr, fi = make_filter_consts()
    PxT, PxrT = make_pext(np.asarray(inp["L1_attn_dist_emb"], np.float32))
    perm = qkv_col_perm()
    aw_p = np.ascontiguousarray(
        np.asarray(inp["L0_attn_attn_w"])[:, :, perm].reshape(4096, 3072))
    lt = np.zeros((128, 256), np.float32)
    for e in range(16):
        lt[:, e * 16: e * 16 + e] = 1.0
    stat = {
        "aw": aw_p,
        "apw": np.ascontiguousarray(np.asarray(inp["L0_attn_aproj_w"]).reshape(4096, 1024)),
        "fcw": np.ascontiguousarray(np.asarray(inp["L0_attn_fc_w"]).reshape(4096, 4096)),
        "mpw": np.ascontiguousarray(np.asarray(inp["L0_attn_mproj_w"]).reshape(4 * 4096, 1024)),
        "Cm": C, "Sm": Sneg,
        "wrT": np.ascontiguousarray(np.asarray(inp["L0_ffn_W_r"]).T),
        "wkT": np.ascontiguousarray(np.asarray(inp["L0_ffn_W_k"]).T),
        "wvT": np.ascontiguousarray(np.asarray(inp["L0_ffn_W_v"]).T),
        "wqT": np.ascontiguousarray(np.asarray(inp["L1_attn_Wq"]).T),
        "wkTq": np.ascontiguousarray(np.asarray(inp["L1_attn_Wk"]).T),
        "wvTq": np.ascontiguousarray(np.asarray(inp["L1_attn_Wv"]).T),
        "bvrow": np.ascontiguousarray(np.asarray(inp["L1_attn_bv"]).reshape(1, 1024)),
        "eyef": np.eye(128, dtype=np.float32),
    }
    statB = {
        "pxT": PxT, "pxrT": PxrT,
        "gwT": np.ascontiguousarray(np.asarray(inp["L1_ffn_gate_weight"]).T),
        "ltmask": lt,
        "gpT": np.ascontiguousarray(np.asarray(inp["L1_ffn_gate_proj"]).transpose(0, 2, 1).reshape(16 * 1024, 512)),
        "upT": np.ascontiguousarray(np.asarray(inp["L1_ffn_up_proj"]).transpose(0, 2, 1).reshape(16 * 1024, 512)),
        "dpT": np.ascontiguousarray(np.asarray(inp["L1_ffn_down_proj"]).transpose(0, 2, 1).reshape(16 * 512, 1024)),
        "eyef": stat["eyef"],
    }
    return stat, statB, fr, fi


def kernel(**inputs):
    inp = {k: np.asarray(v) for k, v in inputs.items()}
    x = inp["L0_attn_x"].astype(np.float32)
    stat, statB, fr, fi = _stage_all(inp)
    ebbt = np.ascontiguousarray(
        np.tile(np.asarray(inp["L1_ffn_e_bias"], np.float32).reshape(1, 16), (128, 1)))
    P = build_full()
    in_maps = []
    for core in range(NCORES):
        b, hh = core // 2, core % 2
        s0 = hh * NREAL
        scols = np.clip(np.arange(WTOK) + s0 - HALO, 0, S - 1)
        m = dict(stat)
        m.update(statB)
        m["pxT"] = np.ascontiguousarray(statB["pxT"][:, s0:s0 + JWC])
        m["pxrT"] = np.ascontiguousarray(statB["pxrT"][:, NREAL - s0:NREAL - s0 + JWC])
        m["pvec"] = build_pvec(inp, core)
        m["ebbt"] = ebbt
        m["x0"] = stage_x_core(x, core)
        m["frt"] = np.ascontiguousarray(fr[scols].T)
        m["fit"] = np.ascontiguousarray(fi[scols].T)
        in_maps.append(m)
    res = run_bass_kernel_spmd(P.nc, in_maps, core_ids=list(range(NCORES)))
    out = np.empty((4, S, E), np.float32)
    for core in range(NCORES):
        b, hh = core // 2, core % 2
        out[b, hh * NREAL:(hh + 1) * NREAL] = res.results[core]["out_scr"].T
    return out
